# version 04 (final)
# speedup vs baseline: 1.2502x; 1.0108x over previous
import numpy as np
from contextlib import ExitStack
import concourse.bass as bass
import concourse.mybir as mybir
from concourse.bass_utils import run_bass_kernel_spmd

F32 = mybir.dt.float32
BF16 = mybir.dt.bfloat16
I32 = mybir.dt.int32
AF = mybir.ActivationFunctionType
ALU = mybir.AluOpType
AX = mybir.AxisListType

NCORES = 8


class Prog:

    def __init__(self, name="prog"):
        self.nc = bass.Bass("TRN2", target_bir_lowering=False)
        self.es = ExitStack()
        self.name = name

    def inp(self, name, shape, dtype=F32):
        return self.nc.dram_tensor(name, list(shape), dtype, kind="ExternalInput").ap()

    def out(self, name, shape, dtype=F32):
        return self.nc.dram_tensor(name, list(shape), dtype, kind="ExternalOutput").ap()

    def dram(self, name, shape, dtype=F32):
        return self.nc.dram_tensor(name, list(shape), dtype, kind="Internal").ap()

    def sb(self, name, shape, dtype=F32):
        return self.es.enter_context(self.nc.sbuf_tensor(name, list(shape), dtype))

    def ps(self, name, shape, dtype=F32):
        return self.es.enter_context(self.nc.psum_tensor(name, list(shape), dtype))

    def sem(self, name):
        return self.es.enter_context(self.nc.semaphore(name))

    def init_consts(self, values):
        self._cvals = list(values)
        self._ctile = self.sb("consts", [128, max(1, len(values))], F32)
        with self.block() as b:
            @b.vector
            def _(vec):
                for j, v in enumerate(self._cvals):
                    vec.memset(self._ctile[:, j:j + 1], float(v))

    def c(self, value):
        j = self._cvals.index(value)
        return self._ctile[:, j:j + 1]

    def block(self):
        return self.nc.Block()

    def close(self):
        self.es.close()


def run(prog, in_maps, core_ids=None, trace=False):
    if core_ids is None:
        core_ids = list(range(len(in_maps)))
    return run_bass_kernel_spmd(prog.nc, in_maps, core_ids=core_ids, trace=trace)


def simrun(prog, in_maps, out_names=None):
    from concourse.bass_interp import CoreSim, MultiCoreSim
    n = len(in_maps)
    if n == 1:
        sims = [CoreSim(prog.nc, require_finite=False, require_nnan=False)]
        top = sims[0]
    else:
        top = MultiCoreSim(prog.nc, num_cores=n, require_finite=False, require_nnan=False)
        sims = list(top.cores.values())
    for cid, cs in enumerate(sims):
        for k, v in in_maps[cid].items():
            cs.tensor(k)[:] = v
    top.simulate()
    if out_names is None:
        out_names = []
        for alloc in prog.nc.m.functions[0].allocations:
            try:
                if alloc.kind == "ExternalOutput":
                    out_names.append(alloc.memorylocations[0].name)
            except AttributeError:
                pass
    return [{k: np.array(cs.tensor(k)) for k in out_names} for cs in sims]



def build_add_ln(n_add, T=2048, D=1024):
    p = Prog("add_ln")
    x = p.inp("x", [T, D])
    adds = [p.inp(nm, [T, D]) for nm in ["a", "f"][:n_add]]
    xnew = p.out("xnew", [T, D]) if n_add else None
    xn = p.out("xn", [T, D])
    nt = T // 128
    G = 4

    xt = p.sb("xt", [128, G * D])
    at = p.sb("at", [128, G * D]) if n_add >= 1 else None
    ft = p.sb("ft", [128, G * D]) if n_add >= 2 else None
    xnt = p.sb("xnt", [128, G * D])
    x2 = p.sb("x2", [128, D])
    stats = p.sb("stats", [128, 2 * G])
    rstd = p.sb("rstd", [128, G])
    ld = p.sem("ld")
    st = p.sem("st")
    p.init_consts([1e-5])

    xr = x.rearrange("(n p) d -> n p d", p=128)
    ar = [a.rearrange("(n p) d -> n p d", p=128) for a in adds]
    xnr = xn.rearrange("(n p) d -> n p d", p=128)
    xwr = xnew.rearrange("(n p) d -> n p d", p=128) if xnew is not None else None
    nld = [0]
    nst = [0]

    for g in range(nt // G):
        with p.block() as b:
            @b.sync
            def _(sync, g=g):
                for j in range(G):
                    i = g * G + j
                    sync.dma_start(out=xt[:, j * D:(j + 1) * D], in_=xr[i]).then_inc(ld, 16); nld[0] += 1
                    if n_add >= 1:
                        sync.dma_start(out=at[:, j * D:(j + 1) * D], in_=ar[0][i]).then_inc(ld, 16); nld[0] += 1
                    if n_add >= 2:
                        sync.dma_start(out=ft[:, j * D:(j + 1) * D], in_=ar[1][i]).then_inc(ld, 16); nld[0] += 1
                sync.wait_ge(ld, 16 * nld[0])

        with p.block() as b:
            @b.vector
            def _(vec):
                for j in range(G):
                    xi = xt[:, j * D:(j + 1) * D]
                    if n_add >= 1:
                        vec.tensor_add(xi, xi, at[:, j * D:(j + 1) * D])
                        vec.drain()
                    if n_add >= 2:
                        vec.tensor_add(xi, xi, ft[:, j * D:(j + 1) * D])
                        vec.drain()
                    vec.tensor_mul(x2[:], xi, xi)
                    vec.reduce_sum(out=stats[:, j:j + 1], in_=xi, axis=AX.X)
                    vec.drain()
                    vec.reduce_sum(out=stats[:, G + j:G + j + 1], in_=x2[:], axis=AX.X)
                    vec.drain()
                vec.tensor_scalar_mul(stats[:, 0:G], stats[:, 0:G], 1.0 / D)
                vec.tensor_scalar_mul(stats[:, G:2 * G], stats[:, G:2 * G], 1.0 / D)
                vec.drain()
                vec.tensor_mul(rstd[:], stats[:, 0:G], stats[:, 0:G])
                vec.drain()
                vec.tensor_sub(stats[:, G:2 * G], stats[:, G:2 * G], rstd[:])

        with p.block() as b:
            @b.scalar
            def _(sca):
                sca.activation(out=rstd[:], in_=stats[:, G:2 * G], func=AF.Sqrt, bias=p.c(1e-5))

        with p.block() as b:
            @b.vector
            def _(vec):
                vec.reciprocal(rstd[:], rstd[:])
                vec.drain()
                for j in range(G):
                    vec.tensor_scalar(out=xnt[:, j * D:(j + 1) * D], in0=xt[:, j * D:(j + 1) * D],
                                      scalar1=stats[:, j:j + 1], scalar2=rstd[:, j:j + 1],
                                      op0=ALU.subtract, op1=ALU.mult)

        with p.block() as b:
            @b.sync
            def _(sync, g=g):
                for j in range(G):
                    i = g * G + j
                    sync.dma_start(out=xnr[i], in_=xnt[:, j * D:(j + 1) * D]).then_inc(st, 16); nst[0] += 1
                    if xwr is not None:
                        sync.dma_start(out=xwr[i], in_=xt[:, j * D:(j + 1) * D]).then_inc(st, 16); nst[0] += 1
                sync.wait_ge(st, 16 * nst[0])
    return p


class MMRes:

    def __init__(self, p, max_nf, tag="", TCH=512):
        self.wt = p.sb(f"wt{tag}", [128, 2 * max_nf * 128], BF16)
        self.ps = [p.ps(f"ps{tag}_{i}", [128, TCH], F32) for i in range(2)]
        self.s_w = p.sem(f"sw{tag}")
        self.s_mm = SC(p, f"smm{tag}")
        self.s_ev = SC(p, f"sev{tag}")
        self.s_wp = [SC(p, f"swp{tag}0"), SC(p, f"swp{tag}1")]
        self.s_wf = SC(p, f"swf{tag}")
        self.cnt = {"w": 0, "mm": 0, "ev": 0}
        self.TCH = TCH


def linear_fm(p, res, x_tiles, T, Din, w_dram, Dout, out_cb, act=None,
              scale_col=None, bias_col=None, evac_dve=False, evac_add=False):
    TCH = res.TCH
    nf = Din // 128
    nco = (Dout + 127) // 128
    ntc = (T + TCH - 1) // TCH
    wt, ps, s_w, s_mm, s_ev, cnt = res.wt, res.ps, res.s_w, res.s_mm, res.s_ev, res.cnt
    assert wt.shape[1] >= 2 * nf * 128, "MMRes wt too small for double buffering"
    fn = (AF.Identity if bias_col is not None else AF.Copy) if act is None else act
    swp = res.s_wp
    s_wf = res.s_wf

    with p.block() as b:
        B_wf = s_wf.v
        B_w = [swp[0].v, swp[1].v]

        @b.gpsimd
        def _(gp):
            for c in range(nco):
                cw = min(128, Dout - c * 128)
                if c >= 2:
                    gp.wait_ge(s_wf.s, B_wf + c - 1)
                par = c % 2
                for f in range(nf):
                    g = gp.dma_start(out=wt[:, (par * nf + f) * 128:(par * nf + f) * 128 + cw],
                                     in_=w_dram[f * 128:(f + 1) * 128, c * 128:c * 128 + cw])
                    swp[par].inc(g, 16)

        B_ev = cnt["ev"]

        @b.tensor
        def _(te):
            ev = B_ev
            for c in range(nco):
                cw = min(128, Dout - c * 128)
                par = c % 2
                te.wait_ge(swp[par].s, B_w[par] + 16 * nf * (c // 2 + 1))
                for t in range(ntc):
                    tw = min(TCH, T - t * TCH)
                    k = c * ntc + t
                    if k >= 2:
                        te.wait_ge(s_ev.s, B_ev + k - 1)
                    for f in range(nf):
                        mm = te.matmul(ps[k % 2][0:cw, 0:tw],
                                       wt[:, (par * nf + f) * 128:(par * nf + f) * 128 + cw],
                                       x_tiles(f)[:, t * TCH:t * TCH + tw],
                                       start=(f == 0), stop=(f == nf - 1))
                    mm.then_inc(s_mm.s, 1)
                    s_mm.v += 1
                te.nop().then_inc(s_wf.s, 1) if False else None

        eng = b.vector if (evac_dve or evac_add) else b.scalar
        B_mm2 = s_mm.v - nco * ntc

        @eng
        def _(sca):
            for c in range(nco):
                cw = min(128, Dout - c * 128)
                for t in range(ntc):
                    tw = min(TCH, T - t * TCH)
                    k = c * ntc + t
                    sca.wait_ge(s_mm.s, B_mm2 + k + 1)
                    if evac_add:
                        dst = out_cb(c, t)[0:cw, 0:tw]
                        inst = sca.tensor_tensor(out=dst, in0=dst, in1=ps[k % 2][0:cw, 0:tw],
                                                 op=ALU.add)
                    elif evac_dve:
                        inst = sca.tensor_copy(out=out_cb(c, t)[0:cw, 0:tw],
                                               in_=ps[k % 2][0:cw, 0:tw])
                    else:
                        kw = {}
                        if scale_col is not None:
                            kw["scale"] = scale_col(c)[0:cw]
                        if bias_col is not None:
                            kw["bias"] = bias_col(c)[0:cw]
                        inst = sca.activation(out=out_cb(c, t)[0:cw, 0:tw],
                                              in_=ps[k % 2][0:cw, 0:tw], func=fn, **kw)
                    inst.then_inc(s_ev.s, 1)
                    s_ev.v += 1
                    cnt["ev"] += 1
                    if t == ntc - 1:
                        sca.sem_inc(s_wf.s, 1)
                        s_wf.v += 1


def load_fm_bf16(p, name, src_ap, Din, T, sem, cnt):
    nf = Din // 128
    tile = p.sb(name, [128, nf * T], BF16)
    with p.block() as b:
        @b.gpsimd
        def _(gp):
            for f in range(nf):
                gp.dma_start(out=tile[:, f * T:(f + 1) * T],
                             in_=src_ap[f * 128:(f + 1) * 128, :]).then_inc(sem, 16)
                cnt[0] += 1
            gp.wait_ge(sem, 16 * cnt[0])
    return tile, (lambda f, tile=tile, T=T: tile[:, f * T:(f + 1) * T])


def load_scale_cols(p, name, src_ap, Dout, sem, cnt):
    nco = Dout // 128
    tile = p.sb(name, [128, nco], F32)
    with p.block() as b:
        @b.sync
        def _(sync):
            sync.dma_start(out=tile[:], in_=src_ap[:, :]).then_inc(sem, 16)
            cnt[0] += 1
            sync.wait_ge(sem, 16 * cnt[0])
    return tile, (lambda c, tile=tile: tile[:, c:c + 1])


def build_l3_ffn(T=2048, D=1024, F=4096, THALF=1024):
    p = Prog("l3ffn")
    xnT = p.inp("xnT", [D, T])
    w1T = p.inp("w1T", [D, F])
    w2T = p.inp("w2T", [D, F])
    h1T = p.inp("h1T", [F, F])
    h2T = p.inp("h2T", [F, F])
    w3T = p.inp("w3T", [F, D])
    s1 = p.inp("s1", [128, F // 128])
    s2 = p.inp("s2", [128, F // 128])
    sh1 = p.inp("sh1", [128, F // 128])
    sh2 = p.inp("sh2", [128, F // 128])
    s3 = p.inp("s3", [128, D // 128])
    yT = p.out("yT", [D, T])

    nfD = D // 128
    nfF = F // 128
    sld = p.sem("sld")
    sldg = p.sem("sldg")
    sst = p.sem("sst")
    cl = [0]
    clg = [0]
    cs = [0]

    xfm = p.sb("xfm", [128, nfD * THALF], BF16)
    U = p.sb("U", [128, nfF * THALF], BF16)
    G = p.sb("G", [128, nfF * THALF], BF16)
    O = p.sb("O", [128, nfD * THALF], F32)
    res = MMRes(p, max_nf=nfF, tag="mm")

    _, s1c = load_scale_cols(p, "s1t", s1, F, sld, cl)
    _, s2c = load_scale_cols(p, "s2t", s2, F, sld, cl)
    _, sh1c = load_scale_cols(p, "sh1t", sh1, F, sld, cl)
    _, sh2c = load_scale_cols(p, "sh2t", sh2, F, sld, cl)
    _, s3c = load_scale_cols(p, "s3t", s3, D, sld, cl)

    for h in range(T // THALF):
        t0 = h * THALF
        with p.block() as b:
            @b.gpsimd
            def _(gp, t0=t0):
                for f in range(nfD):
                    gp.dma_start(out=xfm[:, f * THALF:(f + 1) * THALF],
                                 in_=xnT[f * 128:(f + 1) * 128, t0:t0 + THALF]).then_inc(sldg, 16)
                    clg[0] += 1
                gp.wait_ge(sldg, 16 * clg[0])

        xf = lambda f: xfm[:, f * THALF:(f + 1) * THALF]
        Uf = lambda f: U[:, f * THALF:(f + 1) * THALF]
        Gf = lambda f: G[:, f * THALF:(f + 1) * THALF]
        Of = lambda f: O[:, f * THALF:(f + 1) * THALF]
        linear_fm(p, res, xf, THALF, D, w2T, F, lambda c, t: Uf(c)[:, t * res.TCH:], act=None, scale_col=s2c)
        linear_fm(p, res, xf, THALF, D, w1T, F, lambda c, t: Gf(c)[:, t * res.TCH:], act=AF.Silu, scale_col=s1c)
        with p.block() as b:
            @b.vector
            def _(vec):
                for f in range(nfF):
                    vec.tensor_mul(Gf(f), Gf(f), Uf(f))
                    vec.drain()
        linear_fm(p, res, Gf, THALF, F, h1T, F, lambda c, t: Uf(c)[:, t * res.TCH:], act=AF.Gelu, scale_col=sh1c)
        linear_fm(p, res, Uf, THALF, F, h2T, F, lambda c, t: Gf(c)[:, t * res.TCH:], act=AF.Gelu, scale_col=sh2c)
        linear_fm(p, res, Gf, THALF, F, w3T, D, lambda c, t: Of(c)[:, t * res.TCH:], act=None, scale_col=s3c)
        with p.block() as b:
            @b.sync
            def _(sync, t0=t0):
                for f in range(nfD):
                    sync.dma_start(out=yT[f * 128:(f + 1) * 128, t0:t0 + THALF],
                                   in_=O[:, f * THALF:(f + 1) * THALF]).then_inc(sst, 16)
                    cs[0] += 1
                sync.wait_ge(sst, 16 * cs[0])
    return p


def emit_build_rot128(p, name):
    rt = p.sb(name, [128, 128], BF16)
    di = p.sb(name + "_di", [128, 128], I32)
    pp = p.sb(name + "_pp", [128, 128], I32)
    dif = p.sb(name + "_df", [128, 128], F32)
    pof = p.sb(name + "_po", [128, 128], F32)
    ta = p.sb(name + "_ta", [128, 128], F32)
    tb = p.sb(name + "_tb", [128, 128], F32)
    with p.block() as b:
        @b.gpsimd
        def _(gp):
            gp.iota(di[:], pattern=[[1, 128]], base=0, channel_multiplier=-1)
            gp.iota(pp[:], pattern=[[0, 128]], base=0, channel_multiplier=1)
    with p.block() as b:
        @b.vector
        def _(vec):
            vec.tensor_single_scalar(out=pp[:], in_=pp[:], scalar=1, op=ALU.bitwise_and)
            vec.tensor_copy(out=dif[:], in_=di[:])
            vec.drain()
            vec.tensor_copy(out=pof[:], in_=pp[:])
            vec.tensor_single_scalar(out=ta[:], in_=dif[:], scalar=0.0, op=ALU.is_equal)
            vec.tensor_single_scalar(out=tb[:], in_=dif[:], scalar=1.0, op=ALU.is_equal)
            vec.drain()
            vec.tensor_scalar(out=pof[:], in0=pof[:], scalar1=-1.0, scalar2=1.0, op0=ALU.mult, op1=ALU.add)
            vec.drain()
            vec.tensor_mul(tb[:], tb[:], pof[:])
            vec.drain()
            vec.tensor_add(ta[:], ta[:], tb[:])
            vec.tensor_scalar(out=pof[:], in0=pof[:], scalar1=-1.0, scalar2=1.0, op0=ALU.mult, op1=ALU.add)
            vec.drain()
            vec.tensor_single_scalar(out=tb[:], in_=dif[:], scalar=-1.0, op=ALU.is_equal)
            vec.drain()
            vec.tensor_mul(tb[:], tb[:], pof[:])
            vec.drain()
            vec.tensor_sub(out=rt[:], in0=ta[:], in1=tb[:])
    return rt


class SC:

    def __init__(self, p, name):
        self.s = p.sem(name)
        self.v = 0

    def inc(self, inst, n=1):
        inst.then_inc(self.s, n)
        self.v += n
        return self.v


def build_l2_proj(T=2048, D=1024):
    p = Prog("l2proj")
    xnT = p.inp("xnT", [D, T])
    ws = {nm: p.inp(nm, [D, D]) for nm in ["wqT", "wkT", "wvT"]}
    bs = {nm: p.inp(nm, [128, D // 128]) for nm in ["bq", "bk", "bv"]}
    tfm = p.inp("tfm", [D, T])
    outs = {nm: p.out(nm, [D, T], BF16) for nm in ["qT", "kT", "vT"]}
    nf = D // 128
    sld = p.sem("sld")
    sldg = p.sem("sldg")
    sst = p.sem("sst")
    cl = [0]
    clg = [0]
    cs = [0]

    rt = emit_build_rot128(p, "rot")
    xfm, xf = load_fm_bf16(p, "xfm", xnT, D, T, sldg, clg)
    tb = p.sb("tb", [128, nf * T], BF16)
    with p.block() as b:
        @b.gpsimd
        def _(gp):
            for f in range(nf):
                gp.dma_start(out=tb[:, f * T:(f + 1) * T],
                             in_=tfm[f * 128:(f + 1) * 128, :]).then_inc(sldg, 16)
                clg[0] += 1
            gp.wait_ge(sldg, 16 * clg[0])
    bt = {}
    for nm in ["bq", "bk", "bv"]:
        bt[nm], _ = load_scale_cols(p, nm + "t", bs[nm], D, sld, cl)

    res = MMRes(p, max_nf=nf, tag="mm")
    qkv = p.sb("qkv", [128, 3 * nf * T], BF16)

    def dst(which):
        return lambda c, t: qkv[:, (which * nf + c) * T + t * res.TCH:]

    for i, (wnm, bnm) in enumerate([("wqT", "bq"), ("wkT", "bk"), ("wvT", "bv")]):
        linear_fm(p, res, xf, T, D, ws[wnm], D, dst(i),
                  act=None, bias_col=(lambda c, bn=bnm: bt[bn][:, c:c + 1]))

    smm2 = SC(p, "smm2")
    sev2 = SC(p, "sev2")
    ntc = (T + 511) // 512
    for which in range(2):
        with p.block() as b:
            @b.tensor
            def _(te, which=which):
                ev0 = sev2.v
                k = 0
                for f in range(nf):
                    for t in range(ntc):
                        if k >= 2:
                            te.wait_ge(sev2.s, ev0 + k - 1)
                        tw = min(512, T - t * 512)
                        mm = te.matmul(res.ps[k % 2][:, 0:tw], rt[:],
                                       qkv[:, (which * nf + f) * T + t * 512:(which * nf + f) * T + t * 512 + tw],
                                       start=True, stop=True)
                        smm2.inc(mm)
                        k += 1

            @b.vector
            def _(vec, which=which):
                mm0 = smm2.v - nf * ntc
                k = 0
                for f in range(nf):
                    for t in range(ntc):
                        vec.wait_ge(smm2.s, mm0 + k + 1)
                        tw = min(512, T - t * 512)
                        tt = vec.tensor_tensor(
                            out=qkv[:, (which * nf + f) * T + t * 512:(which * nf + f) * T + t * 512 + tw],
                            in0=res.ps[k % 2][:, 0:tw],
                            in1=tb[:, f * T + t * 512:f * T + t * 512 + tw], op=ALU.mult)
                        sev2.inc(tt)
                        k += 1

    with p.block() as b:
        @b.sync
        def _(sync):
            for i, nm in enumerate(["qT", "kT", "vT"]):
                for f in range(nf):
                    sync.dma_start(out=outs[nm][f * 128:(f + 1) * 128, :],
                                   in_=qkv[:, (i * nf + f) * T:(i * nf + f + 1) * T]).then_inc(sst, 16)
                    cs[0] += 1
            sync.wait_ge(sst, 16 * cs[0])
    return p


def build_l2_flash(TQ=2048, TK=8192, D=1024, scale=1.0 / 512 ** 0.5):
    p = Prog("l2flash")
    qT = p.inp("qT", [D, TQ], BF16)
    kT = p.inp("kT", [D, TK], BF16)
    vtm = p.inp("vtm", [TK, D], BF16)
    y = p.out("y", [TQ, D])
    nf = D // 128
    nqt = TQ // 128
    KC = 512
    nkc = TK // KC

    qfm = p.sb("qfm", [128, nf * TQ], BF16)
    kc = [p.sb(f"kc{i}", [128, nf * KC], BF16) for i in range(2)]
    vc = [p.sb(f"vc{i}", [128, (KC // 128) * D], BF16) for i in range(2)]
    acc = p.sb("acc", [128, nqt * D], F32)
    mrow = p.sb("mrow", [128, nqt], F32)
    nms = p.sb("nms", [128, 1], F32)
    dm = p.sb("dm", [128, 1], F32)
    alpha = p.sb("alpha", [128, 1], F32)
    den = p.sb("den", [128, nqt], F32)
    chs = p.sb("chs", [128, 1], F32)
    rm = p.sb("rm", [128, 1], F32)
    Est = p.sb("Est", [128, KC], BF16)
    ps_s = p.ps("ps_s", [128, KC], F32)
    ps_t = p.ps("ps_t", [128, KC], BF16)
    ps_c = p.ps("ps_c", [128, D], F32)
    idn = p.sb("idn", [128, 128], BF16)
    idn_i = p.sb("idn_i", [128, 128], I32)
    sld = p.sem("sld")
    skl = p.sem("skl")
    sst = p.sem("sst")
    cl = [0]

    with p.block() as b:
        @b.gpsimd
        def _(gp):
            gp.iota(idn_i[:], pattern=[[1, 128]], base=0, channel_multiplier=-1)

        @b.sync
        def _(sync):
            for f in range(nf):
                sync.dma_start(out=qfm[:, f * TQ:(f + 1) * TQ],
                               in_=qT[f * 128:(f + 1) * 128, :]).then_inc(sld, 16)
                cl[0] += 1
            sync.wait_ge(sld, 16 * cl[0])

        @b.vector
        def _(vec):
            vec.memset(acc[:], 0.0)
            vec.memset(mrow[:], -3e38)
            vec.memset(den[:], 0.0)

    with p.block() as b:
        @b.vector
        def _(vec):
            vec.tensor_single_scalar(out=idn[:], in_=idn_i[:], scalar=0, op=ALU.is_equal)

    s_kv = [SC(p, "s_kv0"), SC(p, "s_kv1")]
    s_mm = SC(p, "s_mm")
    s_ex = SC(p, "s_ex")
    s_mx = SC(p, "s_mx")
    s_al = SC(p, "s_al")
    s_tr = SC(p, "s_tr")
    s_ec = SC(p, "s_ec")
    s_av = SC(p, "s_av")
    s_ac = SC(p, "s_ac")

    p.init_consts([scale])

    with p.block() as b:
        @b.sync
        def _(sync):
            for f in range(nf):
                sc_ = sync.dma_start(out=kc[0][:, f * KC:(f + 1) * KC],
                                     in_=kT[f * 128:(f + 1) * 128, 0:KC])
                s_kv[0].inc(sc_, 16)
            for r in range(KC // 128):
                sc_ = sync.dma_start(out=vc[0][:, r * D:(r + 1) * D],
                                     in_=vtm[r * 128:(r + 1) * 128, :])
                s_kv[0].inc(sc_, 16)
            sync.wait_ge(s_kv[0].s, s_kv[0].v)

    for kci in range(nkc):
        cur = kci % 2
        nxt = (kci + 1) % 2
        EB = kci * nqt
        B_kv = s_kv[cur].v
        with p.block() as b:
            if kci + 1 < nkc:
                @b.sync
                def _(sync, kci=kci, nxt=nxt):
                    o = (kci + 1) * KC
                    for f in range(nf):
                        sc_ = sync.dma_start(out=kc[nxt][:, f * KC:(f + 1) * KC],
                                             in_=kT[f * 128:(f + 1) * 128, o:o + KC])
                        s_kv[nxt].inc(sc_, 16)
                    for r in range(KC // 128):
                        sc_ = sync.dma_start(out=vc[nxt][:, r * D:(r + 1) * D],
                                             in_=vtm[(o + r * 128):(o + (r + 1) * 128), :])
                        s_kv[nxt].inc(sc_, 16)

            @b.tensor
            def _(te, kci=kci, cur=cur, EB=EB, B_kv=B_kv):
                te.wait_ge(s_kv[cur].s, B_kv)
                for qt in range(nqt):
                    te.wait_ge(s_ex.s, EB + qt)
                    te.wait_ge(s_mx.s, EB + qt)
                    for f in range(nf):
                        mm = te.matmul(ps_s[:], qfm[:, f * TQ + qt * 128:f * TQ + (qt + 1) * 128],
                                       kc[cur][:, f * KC:(f + 1) * KC],
                                       start=(f == 0), stop=(f == nf - 1))
                    mm.then_inc(s_mm.s, 1); s_mm.v += 1
                    te.wait_ge(s_ex.s, EB + qt + 1)
                    te.wait_ge(s_ec.s, EB + qt)
                    for r in range(KC // 128):
                        tr = te.transpose(ps_t[:, r * 128:(r + 1) * 128], Est[:, r * 128:(r + 1) * 128], idn[:])
                    tr.then_inc(s_tr.s, 1); s_tr.v += 1
                    te.wait_ge(s_ec.s, EB + qt + 1)
                    te.wait_ge(s_ac.s, EB + qt)
                    for r in range(KC // 128):
                        for h in range((D + 511) // 512):
                            hw = min(512, D - h * 512)
                            mm = te.matmul(ps_c[:, h * 512:h * 512 + hw],
                                           Est[:, r * 128:(r + 1) * 128],
                                           vc[cur][:, r * D + h * 512:r * D + h * 512 + hw],
                                           start=(r == 0), stop=(r == KC // 128 - 1))
                    mm.then_inc(s_av.s, 1); s_av.v += 1

            @b.vector
            def _(vec, kci=kci, cur=cur, EB=EB):
                for qt in range(nqt):
                    vec.wait_ge(s_mm.s, EB + qt + 1)
                    vec.wait_ge(s_ex.s, EB + qt)
                    vec.wait_ge(s_al.s, EB + qt)
                    vec.reduce_max(out=rm[:], in_=ps_s[:], axis=AX.X)
                    vec.drain()
                    vec.tensor_scalar_mul(rm[:], rm[:], float(scale))
                    vec.drain()
                    vec.tensor_max(nms[:], mrow[:, qt:qt + 1], rm[:])
                    vec.drain()
                    vec.tensor_sub(dm[:], mrow[:, qt:qt + 1], nms[:])
                    vec.drain()
                    vec.tensor_copy(out=mrow[:, qt:qt + 1], in_=nms[:])
                    vec.drain()
                    iv = vec.tensor_scalar_mul(nms[:], nms[:], -1.0)
                    iv.then_inc(s_mx.s, 1); s_mx.v += 1
                    vec.wait_ge(s_al.s, EB + qt + 1)
                    vec.wait_ge(s_ex.s, EB + qt + 1)
                    vec.scalar_tensor_tensor(out=den[:, qt:qt + 1], in0=den[:, qt:qt + 1],
                                             scalar=alpha[:], in1=chs[:], op0=ALU.mult, op1=ALU.add)
                    vec.wait_ge(s_tr.s, EB + qt + 1)
                    ec = vec.tensor_copy(out=Est[:], in_=ps_t[:])
                    ec.then_inc(s_ec.s, 1); s_ec.v += 1
                    vec.wait_ge(s_av.s, EB + qt + 1)
                    au = vec.scalar_tensor_tensor(out=acc[:, qt * D:(qt + 1) * D], in0=acc[:, qt * D:(qt + 1) * D],
                                                  scalar=alpha[:], in1=ps_c[:], op0=ALU.mult, op1=ALU.add)
                    au.then_inc(s_ac.s, 1); s_ac.v += 1

            @b.scalar
            def _(sca, kci=kci, EB=EB):
                for qt in range(nqt):
                    sca.wait_ge(s_mx.s, EB + qt + 1)
                    al = sca.activation(out=alpha[:], in_=dm[:], func=AF.Exp)
                    al.then_inc(s_al.s, 1); s_al.v += 1
                    sca.wait_ge(s_av.s, EB + qt)
                    ex = sca.activation(out=Est[:], in_=ps_s[:], func=AF.Exp,
                                        scale=p.c(float(scale)), bias=nms[:], accum_out=chs[:])
                    ex.then_inc(s_ex.s, 1); s_ex.v += 1

    with p.block() as b:
        @b.vector
        def _(vec):
            vec.reciprocal(den[:], den[:])
            vec.drain()
            for qt in range(nqt):
                vec.tensor_scalar_mul(acc[:, qt * D:(qt + 1) * D], acc[:, qt * D:(qt + 1) * D],
                                      den[:, qt:qt + 1])

    with p.block() as b:
        @b.sync
        def _(sync):
            n = 0
            yr = y.rearrange("(n p) d -> n p d", p=128)
            for qt in range(nqt):
                sync.dma_start(out=yr[qt], in_=acc[:, qt * D:(qt + 1) * D]).then_inc(sst, 16)
                n += 1
            sync.wait_ge(sst, 16 * n)
    return p


def build_flash_heads(nh, T, dq, dv, scale, causal=True, lobo=False, KC=512, name="flashh"):
    p = Prog(name)
    qT = p.inp("qT", [nh, dq, T], BF16)
    kT = p.inp("kT", [nh, dq, T], BF16)
    vtm = p.inp("vtm", [nh, T, dv], BF16)
    lob = p.inp("lobo", [128, nh]) if lobo else None
    y = p.out("y", [nh, T, dv])
    nqt = T // 128
    nkc = T // KC
    nvr = KC // 128

    qfm = p.sb("qfm", [128, T], BF16)
    kcb = [p.sb(f"kcb{i}", [128, KC], BF16) for i in range(2)]
    vcb = [p.sb(f"vcb{i}", [128, nvr * dv], BF16) for i in range(2)]
    acc = p.sb("acc", [128, nqt * dv], F32)
    mrow = p.sb("mrow", [128, nqt], F32)
    den = p.sb("den", [128, nqt], F32)
    nms = p.sb("nms", [128, 1], F32)
    dm = p.sb("dm", [128, 1], F32)
    alpha = p.sb("alpha", [128, 1], F32)
    chs = p.sb("chs", [128, 1], F32)
    rm = p.sb("rm", [128, 1], F32)
    Est = p.sb("Est", [128, KC], BF16)
    Cl = p.sb("Cl", [128, nh], F32) if lobo else None
    lobs = p.sb("lobs", [128, nh], F32) if lobo else None
    ps_s = p.ps("ps_s", [128, KC], F32)
    ps_t = p.ps("ps_t", [128, KC], BF16)
    ps_c = p.ps("ps_c", [128, dv], F32)
    idn = p.sb("idn", [128, 128], BF16)
    idn_i = p.sb("idn_i", [128, 128], I32)
    msk = p.sb("msk", [128, 2048], BF16)
    msk_i = p.sb("msk_i", [128, 2048], I32)
    msk_f = p.sb("msk_f", [128, 2048], F32)
    sld = p.sem("sld")
    sst = p.sem("sst")
    s_kv = [SC(p, "s_kv0"), SC(p, "s_kv1")]
    s_mm = SC(p, "s_mm")
    s_ex = SC(p, "s_ex")
    s_mx = SC(p, "s_mx")
    s_al = SC(p, "s_al")
    s_tr = SC(p, "s_tr")
    s_ec = SC(p, "s_ec")
    s_av = SC(p, "s_av")
    s_ac = SC(p, "s_ac")
    s_q = SC(p, "s_q")
    s_fin = SC(p, "s_fin")
    p.init_consts([float(scale)])

    with p.block() as b:
        @b.gpsimd
        def _(gp):
            gp.iota(idn_i[:], pattern=[[1, 128]], base=0, channel_multiplier=-1)
            gp.iota(msk_i[:], pattern=[[1, 2048]], base=0, channel_multiplier=-1)

        if lobo:
            @b.sync
            def _(sync):
                sync.dma_start(out=lobs[:], in_=lob[:, :]).then_inc(sld, 16)
                sync.wait_ge(sld, 16)

    with p.block() as b:
        @b.vector
        def _(vec):
            vec.tensor_single_scalar(out=idn[:], in_=idn_i[:], scalar=0, op=ALU.is_equal)
            vec.tensor_single_scalar(out=msk_f[:], in_=msk_i[:], scalar=1024, op=ALU.is_le)
            vec.drain()
            vec.tensor_scalar(out=msk[:], in0=msk_f[:], scalar1=-1.0, scalar2=1e30,
                              op0=ALU.add, op1=ALU.mult)

        if lobo:
            @b.scalar
            def _(sca):
                sca.activation(out=Cl[:], in_=lobs[:], func=AF.Exp)

    for h in range(nh):
        with p.block() as b:
            @b.sync
            def _(sync, h=h):
                sq = sync.dma_start(out=qfm[0:dq, :], in_=qT[h])
                s_q.inc(sq, 16)
                sync.wait_ge(s_q.s, s_q.v)
                sc_ = sync.dma_start(out=kcb[0][0:dq, :], in_=kT[h][:, 0:KC])
                s_kv[0].inc(sc_, 16)
                for r in range(nvr):
                    sc_ = sync.dma_start(out=vcb[0][:, r * dv:(r + 1) * dv],
                                         in_=vtm[h][r * 128:(r + 1) * 128, :])
                    s_kv[0].inc(sc_, 16)

            @b.vector
            def _(vec):
                vec.memset(acc[:], 0.0)
                vec.memset(mrow[:], -3e38)
                vec.memset(den[:], 0.0)

        for kc in range(nkc):
            cur = kc % 2
            nxt = (kc + 1) % 2
            qts = [qt for qt in range((kc * KC) // 128 if causal else 0, nqt)]
            diag = {qt: (causal and qt * 128 < kc * KC + KC) for qt in qts}
            B_mm = s_mm.v
            B_ex = s_ex.v
            B_mx = s_mx.v
            B_al = s_al.v
            B_tr = s_tr.v
            B_ec = s_ec.v
            B_av = s_av.v
            B_ac = s_ac.v
            with p.block() as b:
                B_kv = s_kv[cur].v
                if kc + 1 < nkc:
                    @b.sync
                    def _(sync, h=h, kc=kc, nxt=nxt):
                        hh, o = h, (kc + 1) * KC
                        sc_ = sync.dma_start(out=kcb[nxt][0:dq, :], in_=kT[hh][:, o:o + KC])
                        s_kv[nxt].inc(sc_, 16)
                        for r in range(nvr):
                            sc_ = sync.dma_start(out=vcb[nxt][:, r * dv:(r + 1) * dv],
                                                 in_=vtm[hh][(o + r * 128):(o + (r + 1) * 128), :])
                            s_kv[nxt].inc(sc_, 16)

                @b.tensor
                def _(te, kc=kc, cur=cur, qts=qts, diag=diag, B_kv=B_kv,
                      B_mm=B_mm, B_ex=B_ex, B_mx=B_mx, B_ec=B_ec, B_ac=B_ac):
                    te.wait_ge(s_kv[cur].s, B_kv)
                    for i, qt in enumerate(qts):
                        te.wait_ge(s_ex.s, B_ex + i)
                        te.wait_ge(s_mx.s, B_mx + i)
                        for hh_ in range(KC // 512):
                            sl_ = slice(hh_ * 512, (hh_ + 1) * 512)
                            mm = te.matmul(ps_s[:, sl_], qfm[0:dq, qt * 128:(qt + 1) * 128],
                                           kcb[cur][0:dq, sl_], start=True, stop=not diag[qt])
                            if diag[qt]:
                                d = qt * 128 - kc * KC
                                mm = te.matmul(ps_s[:, sl_], idn[:],
                                               msk[:, 1024 - d + hh_ * 512:1024 - d + (hh_ + 1) * 512],
                                               start=False, stop=True)
                        mm.then_inc(s_mm.s, 1); s_mm.v += 1
                        te.wait_ge(s_ex.s, B_ex + i + 1)
                        te.wait_ge(s_ec.s, B_ec + i)
                        for r in range(nvr):
                            tr = te.transpose(ps_t[:, r * 128:(r + 1) * 128],
                                              Est[:, r * 128:(r + 1) * 128], idn[:])
                        tr.then_inc(s_tr.s, 1); s_tr.v += 1
                        te.wait_ge(s_ec.s, B_ec + i + 1)
                        te.wait_ge(s_ac.s, B_ac + i)
                        for r in range(nvr):
                            mm = te.matmul(ps_c[:], Est[:, r * 128:(r + 1) * 128],
                                           vcb[cur][:, r * dv:(r + 1) * dv],
                                           start=(r == 0), stop=(r == nvr - 1))
                        mm.then_inc(s_av.s, 1); s_av.v += 1

                @b.vector
                def _(vec, qts=qts, B_mm=B_mm, B_ex=B_ex, B_al=B_al, B_tr=B_tr, B_av=B_av):
                    for i, qt in enumerate(qts):
                        vec.wait_ge(s_mm.s, B_mm + i + 1)
                        vec.wait_ge(s_ex.s, B_ex + i)
                        vec.wait_ge(s_al.s, B_al + i)
                        vec.reduce_max(out=rm[:], in_=ps_s[:], axis=AX.X)
                        vec.drain()
                        vec.tensor_scalar_mul(rm[:], rm[:], float(scale))
                        vec.drain()
                        vec.tensor_max(nms[:], mrow[:, qt:qt + 1], rm[:])
                        vec.drain()
                        vec.tensor_sub(dm[:], mrow[:, qt:qt + 1], nms[:])
                        vec.drain()
                        vec.tensor_copy(out=mrow[:, qt:qt + 1], in_=nms[:])
                        vec.drain()
                        iv = vec.tensor_scalar_mul(nms[:], nms[:], -1.0)
                        iv.then_inc(s_mx.s, 1); s_mx.v += 1
                        vec.wait_ge(s_al.s, B_al + i + 1)
                        vec.wait_ge(s_ex.s, B_ex + i + 1)
                        vec.scalar_tensor_tensor(out=den[:, qt:qt + 1], in0=den[:, qt:qt + 1],
                                                 scalar=alpha[:], in1=chs[:], op0=ALU.mult, op1=ALU.add)
                        vec.wait_ge(s_tr.s, B_tr + i + 1)
                        ec = vec.tensor_copy(out=Est[:], in_=ps_t[:])
                        ec.then_inc(s_ec.s, 1); s_ec.v += 1
                        vec.wait_ge(s_av.s, B_av + i + 1)
                        au = vec.scalar_tensor_tensor(out=acc[:, qt * dv:(qt + 1) * dv],
                                                      in0=acc[:, qt * dv:(qt + 1) * dv],
                                                      scalar=alpha[:], in1=ps_c[:],
                                                      op0=ALU.mult, op1=ALU.add)
                        au.then_inc(s_ac.s, 1); s_ac.v += 1

                @b.scalar
                def _(sca, qts=qts, B_mx=B_mx, B_av=B_av):
                    for i, qt in enumerate(qts):
                        sca.wait_ge(s_mx.s, B_mx + i + 1)
                        al = sca.activation(out=alpha[:], in_=dm[:], func=AF.Exp)
                        al.then_inc(s_al.s, 1); s_al.v += 1
                        sca.wait_ge(s_av.s, B_av + i)
                        ex = sca.activation(out=Est[:], in_=ps_s[:], func=AF.Exp,
                                            scale=p.c(float(scale)), bias=nms[:], accum_out=chs[:])
                        ex.then_inc(s_ex.s, 1); s_ex.v += 1

        with p.block() as b:
            @b.vector
            def _(vec, h=h):
                if lobo:
                    vec.tensor_scalar_add(den[:], den[:], Cl[:, h:h + 1])
                    vec.drain()
                vec.reciprocal(den[:], den[:])
                vec.drain()
                for qt in range(nqt):
                    vec.tensor_scalar_mul(acc[:, qt * dv:(qt + 1) * dv],
                                          acc[:, qt * dv:(qt + 1) * dv], den[:, qt:qt + 1])

        with p.block() as b:
            @b.sync
            def _(sync, h=h):
                for qt in range(nqt):
                    sf = sync.dma_start(out=y[h][qt * 128:(qt + 1) * 128, :],
                                        in_=acc[:, qt * dv:(qt + 1) * dv])
                    s_fin.inc(sf, 16)
                sync.wait_ge(s_fin.s, s_fin.v)

    with p.block() as b:
        @b.sync
        def _(sync):
            sync.wait_ge(s_kv[0].s, s_kv[0].v)
            sync.wait_ge(s_kv[1].s, s_kv[1].v)
            sync.wait_ge(s_q.s, s_q.v)
            sync.wait_ge(s_fin.s, s_fin.v)
    return p


def emit_build_pairperm(p, name, half, blk):
    rt = p.sb(name, [128, 128], BF16)
    di = p.sb(name + "_di", [128, 128], I32)
    pp = p.sb(name + "_pp", [128, 128], I32)
    dif = p.sb(name + "_df", [128, 128], F32)
    pof = p.sb(name + "_po", [128, 128], F32)
    ta = p.sb(name + "_ta", [128, 128], F32)
    tb = p.sb(name + "_tb", [128, 128], F32)
    with p.block() as b:
        @b.gpsimd
        def _(gp):
            gp.iota(di[:], pattern=[[1, 128]], base=0, channel_multiplier=-1)
            gp.iota(pp[:], pattern=[[0, 128]], base=0, channel_multiplier=1)
    with p.block() as b:
        @b.vector
        def _(vec):
            vec.tensor_single_scalar(out=pp[:], in_=pp[:], scalar=blk - 1, op=ALU.bitwise_and)
            vec.tensor_copy(out=dif[:], in_=di[:])
            vec.drain()
            vec.tensor_copy(out=pof[:], in_=pp[:])
            vec.drain()
            vec.tensor_single_scalar(out=pof[:], in_=pof[:], scalar=float(half) - 0.5, op=ALU.is_lt)
            vec.tensor_single_scalar(out=ta[:], in_=dif[:], scalar=float(half), op=ALU.is_equal)
            vec.drain()
            vec.tensor_mul(ta[:], ta[:], pof[:])
            vec.tensor_single_scalar(out=tb[:], in_=dif[:], scalar=-float(half), op=ALU.is_equal)
            vec.drain()
            vec.tensor_scalar(out=pof[:], in0=pof[:], scalar1=-1.0, scalar2=1.0, op0=ALU.mult, op1=ALU.add)
            vec.drain()
            vec.tensor_mul(tb[:], tb[:], pof[:])
            vec.drain()
            vec.tensor_sub(out=rt[:], in0=ta[:], in1=tb[:])
    return rt


def emit_rope_pairs(p, res, permT, buf, nchunks, T, cost, sint, scratch, sems):
    smm, sev = sems
    ntc = (T + 511) // 512
    with p.block() as b:
        @b.tensor
        def _(te):
            ev0 = sev.v
            k = 0
            for f in range(nchunks):
                for t in range(ntc):
                    if k >= 2:
                        te.wait_ge(sev.s, ev0 + k - 1)
                    tw = min(512, T - t * 512)
                    mm = te.matmul(res.ps[k % 2][:, 0:tw], permT[:],
                                   buf[:, f * T + t * 512:f * T + t * 512 + tw],
                                   start=True, stop=True)
                    mm.then_inc(smm.s, 1); smm.v += 1
                    k += 1

        @b.vector
        def _(vec):
            mm0 = smm.v - nchunks * ntc
            k = 0
            for f in range(nchunks):
                for t in range(ntc):
                    vec.wait_ge(smm.s, mm0 + k + 1)
                    tw = min(512, T - t * 512)
                    sl = slice(f * T + t * 512, f * T + t * 512 + tw)
                    tsl = slice(t * 512, t * 512 + tw)
                    vec.tensor_tensor(out=scratch[:, 0:tw], in0=res.ps[k % 2][:, 0:tw],
                                      in1=sint[:, tsl], op=ALU.mult)
                    vec.drain()
                    vec.tensor_tensor(out=buf[:, sl], in0=buf[:, sl], in1=cost[:, tsl], op=ALU.mult)
                    vec.drain()
                    tt = vec.tensor_add(buf[:, sl], buf[:, sl], scratch[:, 0:tw])
                    tt.then_inc(sev.s, 1); sev.v += 1
                    vec.drain()
                    k += 1


def build_l0_proj(T=2048, D=1024):
    p = Prog("l0proj")
    xnT = p.inp("xnT", [D, T])
    wq = p.inp("wq", [D, D])
    wqr = p.inp("wqr", [D, 512])
    wkr = p.inp("wkr", [D, 32])
    wkvd = p.inp("wkvd", [D, 256])
    wku = p.inp("wku", [256, D])
    wvu = p.inp("wvu", [256, D])
    cosq = p.inp("cosq", [128, T])
    sinq = p.inp("sinq", [128, T])
    qcT = p.out("qcT", [D, T], BF16)
    qrT = p.out("qrT", [512, T], BF16)
    krT = p.out("krT", [128, T], BF16)
    kcT = p.out("kcT", [D, T], BF16)
    vT = p.out("vT", [D, T], BF16)
    nf = D // 128
    sldg = p.sem("sldg")
    sst = p.sem("sst")
    clg = [0]
    cs = [0]

    permT = emit_build_pairperm(p, "prm", 16, 32)
    xfm, xf = load_fm_bf16(p, "xfm", xnT, D, T, sldg, clg)
    cost = p.sb("cost", [128, T], BF16)
    sint = p.sb("sint", [128, T], BF16)
    with p.block() as b:
        @b.gpsimd
        def _(gp):
            gp.dma_start(out=cost[:], in_=cosq[:, :]).then_inc(sldg, 16)
            gp.dma_start(out=sint[:], in_=sinq[:, :]).then_inc(sldg, 16)
            clg[0] += 2
            gp.wait_ge(sldg, 16 * clg[0])

    res = MMRes(p, max_nf=nf, tag="mm")
    qc = p.sb("qc", [128, nf * T], BF16)
    qr = p.sb("qr", [128, 4 * T], BF16)
    kr = p.sb("kr", [128, 1 * T], BF16)
    lat = p.sb("lat", [128, 2 * T], BF16)
    kc = p.sb("kc", [128, nf * T], BF16)
    vv = p.sb("vv", [128, nf * T], BF16)
    scr = p.sb("scr", [128, 512], F32)

    with p.block() as b:
        @b.vector
        def _(vec):
            vec.memset(kr[:], 0.0)

    mk = lambda buf: (lambda c, t, buf=buf: buf[:, c * T + t * res.TCH:])
    linear_fm(p, res, xf, T, D, wq, D, mk(qc))
    linear_fm(p, res, xf, T, D, wqr, 512, mk(qr))
    linear_fm(p, res, xf, T, D, wkr, 32, mk(kr))
    linear_fm(p, res, xf, T, D, wkvd, 256, mk(lat))
    latf = lambda f: lat[:, f * T:(f + 1) * T]
    linear_fm(p, res, latf, T, 256, wku, D, mk(kc))
    linear_fm(p, res, latf, T, 256, wvu, D, mk(vv))

    smm2 = SC(p, "smm2")
    sev2 = SC(p, "sev2")
    emit_rope_pairs(p, res, permT, qr, 4, T, cost, sint, scr, (smm2, sev2))
    emit_rope_pairs(p, res, permT, kr, 1, T, cost, sint, scr, (smm2, sev2))

    with p.block() as b:
        @b.sync
        def _(sync):
            for f in range(nf):
                sync.dma_start(out=qcT[f * 128:(f + 1) * 128, :], in_=qc[:, f * T:(f + 1) * T]).then_inc(sst, 16); cs[0] += 1
                sync.dma_start(out=kcT[f * 128:(f + 1) * 128, :], in_=kc[:, f * T:(f + 1) * T]).then_inc(sst, 16); cs[0] += 1
                sync.dma_start(out=vT[f * 128:(f + 1) * 128, :], in_=vv[:, f * T:(f + 1) * T]).then_inc(sst, 16); cs[0] += 1
            for f in range(4):
                sync.dma_start(out=qrT[f * 128:(f + 1) * 128, :], in_=qr[:, f * T:(f + 1) * T]).then_inc(sst, 16); cs[0] += 1
            sync.dma_start(out=krT[:, :], in_=kr[:, 0:T]).then_inc(sst, 16); cs[0] += 1
            sync.wait_ge(sst, 16 * cs[0])
    return p


def build_linear(T=2048, Din=1024, Dout=1024, act=None, bias=False, out_f32=True, in_f32=True):
    p = Prog("lin")
    xT = p.inp("xT", [Din, T], F32 if in_f32 else BF16)
    w = p.inp("w", [Din, Dout])
    bvec = p.inp("b", [128, Dout // 128]) if bias else None
    yT = p.out("yT", [Dout, T], F32 if out_f32 else BF16)
    sld = p.sem("sld")
    sldg = p.sem("sldg")
    sst = p.sem("sst")
    cl = [0]
    clg = [0]
    nf = Din // 128
    nco = Dout // 128
    if in_f32:
        xfm, xf = load_fm_bf16(p, "xfm", xT, Din, T, sldg, clg)
    else:
        xfm = p.sb("xfm", [128, nf * T], BF16)
        with p.block() as b:
            @b.sync
            def _(sync):
                for f in range(nf):
                    sync.dma_start(out=xfm[:, f * T:(f + 1) * T],
                                   in_=xT[f * 128:(f + 1) * 128, :]).then_inc(sld, 16)
                    cl[0] += 1
                sync.wait_ge(sld, 16 * cl[0])
        xf = lambda f: xfm[:, f * T:(f + 1) * T]
    bcol = None
    if bias:
        _, bcol = load_scale_cols(p, "bt", bvec, Dout, sld, cl)
    ob = p.sb("ob", [128, nco * T], F32 if out_f32 else BF16)
    res = MMRes(p, max_nf=nf, tag="mm")
    linear_fm(p, res, xf, T, Din, w, Dout, lambda c, t: ob[:, c * T + t * res.TCH:],
              act=act, bias_col=bcol)
    with p.block() as b:
        @b.sync
        def _(sync):
            n = 0
            for c in range(nco):
                sync.dma_start(out=yT[c * 128:(c + 1) * 128, :], in_=ob[:, c * T:(c + 1) * T]).then_inc(sst, 16)
                n += 1
            sync.wait_ge(sst, 16 * n)
    return p


def emit_fp4_quant_block(p, src_tiles, n_tiles, W, dst_tiles, sA, sB, Wmax=2048):
    nb = W // 16
    if not hasattr(p, "_fpq"):
        p._fpq = dict(
            a=p.sb("fpq_a", [128, Wmax], F32),
            h=p.sb("fpq_h", [128, Wmax // 2], F32),
            q=p.sb("fpq_q", [128, Wmax], F32),
            s=p.sb("fpq_s", [128, Wmax], F32),
            r=p.sb("fpq_r", [128, Wmax // 16], F32),
            sc=p.sb("fpq_sc", [128, Wmax // 16], F32),
        )
    assert W <= p._fpq["a"].shape[1]
    scr_a = p._fpq["a"][:, 0:W]
    scr_h = p._fpq["h"][:, 0:W // 2]
    scr_q = p._fpq["q"][:, 0:W]
    scr_s = p._fpq["s"][:, 0:W]
    rec = p._fpq["r"][:, 0:nb]
    scb = p._fpq["sc"][:, 0:nb]
    for i in range(n_tiles):
        with p.block() as b:
            @b.scalar
            def _(sca, i=i):
                sca.activation(out=scr_a, in_=src_tiles(i), func=AF.Abs)
                sca.drain()
                sca.activation(out=scr_s, in_=src_tiles(i), func=AF.Sign).then_inc(sA.s, 1)
                sA.v += 1

            @b.vector
            def _(vec, i=i):
                vec.wait_ge(sA.s, sA.v)
                a3 = scr_a.rearrange("p (g j) -> p g j", j=16)
                h3 = scr_h.rearrange("p (g j) -> p g j", j=8)
                vec.tensor_max(h3, a3[:, :, 0:8], a3[:, :, 8:16])
                vec.drain()
                vec.tensor_max(h3[:, :, 0:4], h3[:, :, 0:4], h3[:, :, 4:8])
                vec.drain()
                vec.tensor_max(h3[:, :, 0:2], h3[:, :, 0:2], h3[:, :, 2:4])
                vec.drain()
                vec.tensor_max(h3[:, :, 0:1], h3[:, :, 0:1], h3[:, :, 1:2])
                vec.drain()
                vec.tensor_scalar(out=scb, in0=h3[:, :, 0], scalar1=1.0 / 6.0,
                                  scalar2=1e-12, op0=ALU.mult, op1=ALU.max)
                vec.drain()
                vec.reciprocal(rec, scb)
                vec.drain()
                rb = rec.unsqueeze(2).broadcast_to([128, nb, 16])
                vec.tensor_tensor(out=a3, in0=a3, in1=rb, op=ALU.mult)
                vec.drain()
                A = scr_a
                Q = scr_q
                vec.tensor_single_scalar(out=Q, in_=A, scalar=0.25, op=ALU.is_gt)
                vec.drain()
                for thr in (0.75, 1.25, 1.75):
                    vec.scalar_tensor_tensor(out=Q, in0=A, scalar=thr, in1=Q,
                                             op0=ALU.is_gt, op1=ALU.add)
                    vec.drain()
                vec.tensor_scalar_mul(Q, Q, 0.5)
                vec.drain()
                for thr in (2.5, 3.5):
                    vec.scalar_tensor_tensor(out=Q, in0=A, scalar=thr, in1=Q,
                                             op0=ALU.is_gt, op1=ALU.add)
                    vec.drain()
                for hh in range(2):
                    sl = slice(hh * W // 2, (hh + 1) * W // 2)
                    vec.tensor_scalar(out=scr_h[:, 0:W // 2], in0=A[:, sl],
                                      scalar1=5.0, scalar2=2.0, op0=ALU.is_gt, op1=ALU.mult)
                    vec.drain()
                    vec.tensor_add(Q[:, sl], Q[:, sl], scr_h[:, 0:W // 2])
                    vec.drain()
                vec.tensor_mul(Q, Q, scr_s)
                vec.drain()
                q3 = scr_q.rearrange("p (g j) -> p g j", j=16)
                d3 = dst_tiles(i).rearrange("p (g j) -> p g j", j=16)
                scb3 = scb.unsqueeze(2).broadcast_to([128, nb, 16])
                tt = vec.tensor_tensor(out=d3, in0=q3, in1=scb3, op=ALU.mult)
                tt.then_inc(sB.s, 1)
                sB.v += 1


_P2CNT = [0]


def p2id(p):
    _P2CNT[0] += 1
    return str(_P2CNT[0])


def build_l0_ffn(T=2048, D=1024, I=2816):
    p = Prog("l0ffn")
    xn = p.inp("xn", [T, D])
    wg = p.inp("wg", [I, D])
    wu = p.inp("wu", [I, D])
    wd = p.inp("wd", [D, I])
    y = p.out("y", [T, D])
    wgq_h = p.dram("wgq_h", [D, I], BF16)
    wuq_h = p.dram("wuq_h", [D, I], BF16)
    wdq_h = p.dram("wdq_h", [I, D], BF16)
    nt = T // 128
    nfD = D // 128
    nfI = I // 128
    nq = 4
    TQ = T // nq
    ntq = TQ // 128
    NCS = [512] * 5 + [256]
    assert sum(NCS) == I

    sA = SC(p, "sA")
    sB = SC(p, "sB")
    sld = SC(p, "sld")
    sst = SC(p, "sst")
    sstp = [SC(p, "sstp0"), SC(p, "sstp1")]
    s_tp = SC(p, "s_tp")
    s_tc = SC(p, "s_tc")
    smm = SC(p, "smm")
    sev = SC(p, "sev")
    ssl = SC(p, "ssl")
    smul = SC(p, "smul")
    shq = SC(p, "shq")

    xt = p.sb("xt", [128, D], F32)
    xq_tm = p.sb("xq_tm", [128, D], BF16)
    xq_fm = p.sb("xq_fm", [128, nfD * T], BF16)
    wrow = p.sb("wrow", [128, I], F32)
    wq_tm = p.sb("wq_tm", [128, I], BF16)
    hbuf = p.sb("hbuf", [128, ntq * I], BF16)
    hq = p.sb("hq", [128, ntq * I], BF16)
    hq_fm = p.sb("hq_fm", [128, nfI * TQ], BF16)
    wnc = p.sb("wnc", [128, 2 * nfD * 512], BF16)
    wdq = p.sb("wdq", [128, nfI * 512], BF16)
    scr = p.sb("scr", [128, 512], F32)
    outb = p.sb("outb", [128, 2 * D], F32)
    idn = p.sb("idn", [128, 128], BF16)
    idn_i = p.sb("idn_i", [128, 128], I32)
    ps_a = p.ps("ps_a", [128, 512], F32)
    ps_b = p.ps("ps_b", [128, 512], F32)
    ps_t = p.ps("ps_t", [128, 2048], BF16)
    tstage = p.sb("tstage", [128, 256], BF16)

    with p.block() as b:
        @b.gpsimd
        def _(gp):
            gp.iota(idn_i[:], pattern=[[1, 128]], base=0, channel_multiplier=-1)
    with p.block() as b:
        @b.vector
        def _(vec):
            vec.tensor_single_scalar(out=idn[:], in_=idn_i[:], scalar=0, op=ALU.is_equal)

    for t in range(nt):
        with p.block() as b:
            @b.sync
            def _(sync, t=t):
                s = sync.dma_start(out=xt[:], in_=xn[t * 128:(t + 1) * 128, :])
                sld.inc(s, 16)
                sync.wait_ge(sld.s, sld.v)
        emit_fp4_quant_block(p, lambda i: xt[:], 1, D, lambda i: xq_tm[:], sA, sB)
        with p.block() as b:
            @b.tensor
            def _(te, t=t):
                for c in range(nfD):
                    if c >= 2:
                        te.wait_ge(s_tc.s, s_tc.v + c - 1)
                    tr = te.transpose(ps_t[:, (c % 2) * 1024:(c % 2) * 1024 + 128],
                                      xq_tm[:, c * 128:(c + 1) * 128], idn[:])
                    tr.then_inc(s_tp.s, 1); s_tp.v += 1

            @b.vector
            def _(vec, t=t):
                tp0 = s_tp.v - nfD
                for c in range(nfD):
                    vec.wait_ge(s_tp.s, tp0 + c + 1)
                    cp = vec.tensor_copy(out=xq_fm[:, c * T + t * 128:c * T + (t + 1) * 128],
                                         in_=ps_t[:, (c % 2) * 1024:(c % 2) * 1024 + 128])
                    cp.then_inc(s_tc.s, 1); s_tc.v += 1

    def prep_weight(w_src, n_tiles, W, dst_h):
        nfW = W // 128
        for t in range(n_tiles):
            with p.block() as b:
                @b.sync
                def _(sync, t=t):
                    s = sync.dma_start(out=(wrow[:, 0:W] if W == I else xt[:]),
                                       in_=w_src[t * 128:(t + 1) * 128, :])
                    sld.inc(s, 16)
                    sync.wait_ge(sld.s, sld.v)
            src_t = (lambda i: wrow[:, 0:W]) if W == I else (lambda i: xt[:])
            dst_t = (lambda i: wq_tm[:, 0:W])
            if W == I:
                for hh in range(2):
                    emit_fp4_quant_block(p, lambda i, hh=hh: wrow[:, hh * (W // 2):(hh + 1) * (W // 2)],
                                         1, W // 2, lambda i, hh=hh: wq_tm[:, hh * (W // 2):(hh + 1) * (W // 2)],
                                         sA, sB)
            else:
                emit_fp4_quant_block(p, src_t, 1, W, dst_t, sA, sB)
            with p.block() as b:
                @b.tensor
                def _(te, t=t):
                    for c in range(nfW):
                        if c >= 2:
                            te.wait_ge(s_tc.s, s_tc.v + c - 1)
                        tr = te.transpose(ps_t[:, (c % 2) * 1024:(c % 2) * 1024 + 128],
                                          wq_tm[:, c * 128:(c + 1) * 128], idn[:])
                        tr.then_inc(s_tp.s, 1); s_tp.v += 1

                @b.vector
                def _(vec, t=t):
                    tp0 = s_tp.v - nfW
                    stb = [sstp[0].v, sstp[1].v]
                    for c in range(nfW):
                        vec.wait_ge(s_tp.s, tp0 + c + 1)
                        if c >= 2:
                            vec.wait_ge(sstp[c % 2].s, stb[c % 2] + 16 * (c // 2))
                        cp = vec.tensor_copy(out=tstage[:, (c % 2) * 128:(c % 2) * 128 + 128],
                                             in_=ps_t[:, (c % 2) * 1024:(c % 2) * 1024 + 128])
                        cp.then_inc(s_tc.s, 1); s_tc.v += 1

                @b.gpsimd
                def _(gp, t=t):
                    tc0 = s_tc.v - nfW
                    for c in range(nfW):
                        gp.wait_ge(s_tc.s, tc0 + c + 1)
                        s = gp.dma_start(out=dst_h[c * 128:(c + 1) * 128, t * 128:(t + 1) * 128],
                                         in_=tstage[:, (c % 2) * 128:(c % 2) * 128 + 128])
                        sstp[c % 2].inc(s, 16)
                    gp.wait_ge(sstp[0].s, sstp[0].v)
                    gp.wait_ge(sstp[1].s, sstp[1].v)

    prep_weight(wg, nfI, D, wgq_h)
    prep_weight(wu, nfI, D, wuq_h)
    prep_weight(wd, nfD, I, wdq_h)

    for q in range(nq):
        off = 0
        for nci, NW in enumerate(NCS):
            with p.block() as b:
                B_ld = sld.v
                @b.sync
                def _(sync, off=off, NW=NW):
                    for f in range(nfD):
                        s = sync.dma_start(out=wnc[:, f * 512:f * 512 + NW],
                                           in_=wuq_h[f * 128:(f + 1) * 128, off:off + NW])
                        sld.inc(s, 16)
                    for f in range(nfD):
                        s = sync.dma_start(out=wnc[:, (nfD + f) * 512:(nfD + f) * 512 + NW],
                                           in_=wgq_h[f * 128:(f + 1) * 128, off:off + NW])
                        sld.inc(s, 16)

                B_mm = smm.v
                B_ev = sev.v
                B_sl = ssl.v
                B_mu = smul.v

                @b.tensor
                def _(te, q=q, off=off, NW=NW, B_ld=B_ld, B_ev=B_ev, B_sl=B_sl):
                    te.wait_ge(sld.s, B_ld + 16 * 2 * nfD)
                    for j in range(ntq):
                        tok = (q * ntq + j) * 128
                        if j >= 1:
                            te.wait_ge(sev.s, B_ev + j)
                            te.wait_ge(ssl.s, B_sl + j)
                        for f in range(nfD):
                            mm = te.matmul(ps_a[:, 0:NW], xq_fm[:, f * T + tok:f * T + tok + 128],
                                           wnc[:, f * 512:f * 512 + NW],
                                           start=(f == 0), stop=(f == nfD - 1))
                        mm.then_inc(smm.s, 1); smm.v += 1
                        for f in range(nfD):
                            mm = te.matmul(ps_b[:, 0:NW], xq_fm[:, f * T + tok:f * T + tok + 128],
                                           wnc[:, (nfD + f) * 512:(nfD + f) * 512 + NW],
                                           start=(f == 0), stop=(f == nfD - 1))
                        mm.then_inc(smm.s, 1); smm.v += 1

                @b.scalar
                def _(sca, off=off, NW=NW, B_mm=B_mm, B_mu=B_mu):
                    for j in range(ntq):
                        sca.wait_ge(smm.s, B_mm + 2 * j + 1)
                        cp = sca.activation(out=hbuf[:, j * I + off:j * I + off + NW],
                                            in_=ps_a[:, 0:NW], func=AF.Copy)
                        cp.then_inc(sev.s, 1); sev.v += 1
                        sca.wait_ge(smm.s, B_mm + 2 * j + 2)
                        if j >= 1:
                            sca.wait_ge(smul.s, B_mu + j)
                        sl = sca.activation(out=scr[:, 0:NW], in_=ps_b[:, 0:NW], func=AF.Silu)
                        sl.then_inc(ssl.s, 1); ssl.v += 1

                @b.vector
                def _(vec, off=off, NW=NW, B_ev=B_ev, B_sl=B_sl):
                    for j in range(ntq):
                        vec.wait_ge(sev.s, B_ev + j + 1)
                        vec.wait_ge(ssl.s, B_sl + j + 1)
                        sl_ = slice(j * I + off, j * I + off + NW)
                        tt = vec.tensor_tensor(out=hbuf[:, sl_], in0=hbuf[:, sl_],
                                               in1=scr[:, 0:NW], op=ALU.mult)
                        tt.then_inc(smul.s, 1); smul.v += 1
            off += NW

        for j in range(ntq):
            for hh in range(2):
                emit_fp4_quant_block(p, lambda i, j=j, hh=hh: hbuf[:, j * I + hh * (I // 2):j * I + (hh + 1) * (I // 2)],
                                     1, I // 2,
                                     lambda i, j=j, hh=hh: hq[:, j * I + hh * (I // 2):j * I + (hh + 1) * (I // 2)],
                                     sA, sB)

        with p.block() as b:
            @b.tensor
            def _(te, q=q):
                k = 0
                for j in range(ntq):
                    for c in range(nfI):
                        if k >= 2:
                            te.wait_ge(s_tc.s, s_tc.v + k - 1)
                        tr = te.transpose(ps_t[:, (k % 2) * 1024:(k % 2) * 1024 + 128],
                                          hq[:, j * I + c * 128:j * I + (c + 1) * 128], idn[:])
                        tr.then_inc(s_tp.s, 1); s_tp.v += 1
                        k += 1

            @b.vector
            def _(vec, q=q):
                tp0 = s_tp.v - ntq * nfI
                k = 0
                for j in range(ntq):
                    for c in range(nfI):
                        vec.wait_ge(s_tp.s, tp0 + k + 1)
                        cp = vec.tensor_copy(out=hq_fm[:, c * TQ + j * 128:c * TQ + (j + 1) * 128],
                                             in_=ps_t[:, (k % 2) * 1024:(k % 2) * 1024 + 128])
                        cp.then_inc(s_tc.s, 1); s_tc.v += 1
                        k += 1

        for half in range(2):
            with p.block() as b:
                B_ld = sld.v
                @b.sync
                def _(sync, half=half):
                    for c in range(nfI):
                        s = sync.dma_start(out=wdq[:, c * 512:(c + 1) * 512],
                                           in_=wdq_h[c * 128:(c + 1) * 128, half * 512:(half + 1) * 512])
                        sld.inc(s, 16)

                B_mm = smm.v
                B_hq = shq.v
                B_st = 0

                @b.tensor
                def _(te, q=q, half=half, B_ld=B_ld, B_hq=B_hq):
                    te.wait_ge(sld.s, B_ld + 16 * nfI)
                    for j in range(ntq):
                        if j >= 2:
                            te.wait_ge(shq.s, B_hq + j - 1)
                        for c in range(nfI):
                            mm = te.matmul([ps_a, ps_b][j % 2][:, 0:512],
                                           hq_fm[:, c * TQ + j * 128:c * TQ + (j + 1) * 128],
                                           wdq[:, c * 512:(c + 1) * 512],
                                           start=(c == 0), stop=(c == nfI - 1))
                        mm.then_inc(smm.s, 1); smm.v += 1

                @b.vector
                def _(vec, q=q, half=half, B_mm=B_mm, B_st=B_st):
                    stb = [sstp[0].v, sstp[1].v]
                    for j in range(ntq):
                        vec.wait_ge(smm.s, B_mm + j + 1)
                        if j >= 2:
                            vec.wait_ge(sstp[j % 2].s, stb[j % 2] + 16 * (j // 2))
                        cp = vec.tensor_copy(out=outb[:, (j % 2) * D + half * 512:(j % 2) * D + half * 512 + 512],
                                             in_=[ps_a, ps_b][j % 2][:, 0:512])
                        cp.then_inc(shq.s, 1); shq.v += 1

                @b.gpsimd
                def _(gp, q=q, half=half, B_hq=B_hq):
                    for j in range(ntq):
                        gp.wait_ge(shq.s, B_hq + j + 1)
                        tok = (q * ntq + j) * 128
                        s = gp.dma_start(out=y[tok:tok + 128, half * 512:(half + 1) * 512],
                                         in_=outb[:, (j % 2) * D + half * 512:(j % 2) * D + half * 512 + 512])
                        sstp[j % 2].inc(s, 16)
                    gp.wait_ge(sstp[0].s, sstp[0].v)
                    gp.wait_ge(sstp[1].s, sstp[1].v)
    return p


def emit_rmsnorm_fm(p, res, x_tiles, out_tiles, T, D, wcol, eps, sems, x2s, stats_sb, rrow, rstd_fm, out_dtype_note=None):
    nf = D // 128
    nt = T // 128
    sA, sB = sems
    for f in range(nf):
        with p.block() as b:
            @b.scalar
            def _(sca, f=f):
                sq = sca.activation(out=x2s[:], in_=x_tiles(f), func=AF.Square)
                sq.then_inc(sA.s, 1)
                sA.v += 1
            B_A = sA.v

            @b.tensor
            def _(te, f=f, B_A=B_A):
                te.wait_ge(sA.s, B_A)
                for t in range(nt):
                    te.matmul(res.ps[1][:, t:t + 1], x2s[:, t * 128:(t + 1) * 128], p._ones[:],
                              start=(f == 0), stop=(f == nf - 1))

    with p.block() as b:
        @b.vector
        def _(vec):
            vec.tensor_scalar(out=stats_sb[:, 0:nt], in0=res.ps[1][:, 0:nt], scalar1=1.0 / D,
                              scalar2=float(eps), op0=ALU.mult, op1=ALU.add)
    with p.block() as b:
        @b.scalar
        def _(sca):
            sca.activation(out=stats_sb[:, nt:2 * nt], in_=stats_sb[:, 0:nt], func=AF.Sqrt)
    with p.block() as b:
        @b.vector
        def _(vec):
            vec.reciprocal(stats_sb[:, nt:2 * nt], stats_sb[:, nt:2 * nt])

    with p.block() as b:
        B_B = sB.v
        @b.tensor
        def _(te, B_B=B_B):
            for t in range(nt):
                if t >= 1:
                    te.wait_ge(sB.s, B_B + t)
                te.matmul(res.ps[0][0:1, 0:128], stats_sb[:, nt + t:nt + t + 1], p._idn_f32[:],
                          start=True, stop=True).then_inc(sA.s, 1)
                sA.v += 1
        B_A2 = sA.v

        @b.vector
        def _(vec, B_A2=B_A2):
            m0 = B_A2 - nt
            for t in range(nt):
                vec.wait_ge(sA.s, m0 + t + 1)
                cp = vec.tensor_copy(out=rrow[0:1, t * 128:(t + 1) * 128], in_=res.ps[0][0:1, 0:128])
                cp.then_inc(sB.s, 1)
                sB.v += 1

    with p.block() as b:
        B_B2 = sB.v
        @b.tensor
        def _(te, B_B2=B_B2):
            for t in range(T // 512):
                if t >= 2:
                    te.wait_ge(sB.s, B_B2 + t - 1)
                te.matmul(res.ps[t % 2][:, 0:512], p._ones1x128f[0:1, 0:128], rrow[0:1, t * 512:(t + 1) * 512],
                          start=True, stop=True).then_inc(sA.s, 1)
                sA.v += 1
        B_A3 = sA.v

        @b.vector
        def _(vec, B_A3=B_A3):
            m0 = B_A3 - T // 512
            for t in range(T // 512):
                vec.wait_ge(sA.s, m0 + t + 1)
                cp = vec.tensor_copy(out=rstd_fm[:, t * 512:(t + 1) * 512], in_=res.ps[t % 2][:, 0:512])
                cp.then_inc(sB.s, 1)
                sB.v += 1

    with p.block() as b:
        @b.vector
        def _(vec):
            for f in range(nf):
                if wcol is not None:
                    vec.scalar_tensor_tensor(out=out_tiles(f), in0=x_tiles(f), scalar=wcol(f),
                                             in1=rstd_fm[:], op0=ALU.mult, op1=ALU.mult)
                else:
                    vec.tensor_tensor(out=out_tiles(f), in0=x_tiles(f), in1=rstd_fm[:], op=ALU.mult)
                vec.drain()


def emit_common_consts(p):
    p._ones = p.sb("c_ones", [128, 1], F32)
    p._ones1x128 = p.sb("c_ones1", [1, 128], BF16)
    p._ones1x128f = p.sb("c_ones1f", [1, 128], F32)
    p._idn_f32 = p.sb("c_idnf", [128, 128], F32)
    p._idn_bf = p.sb("c_idnb", [128, 128], BF16)
    ii = p.sb("c_ii", [128, 128], I32)
    with p.block() as b:
        @b.gpsimd
        def _(gp):
            gp.iota(ii[:], pattern=[[1, 128]], base=0, channel_multiplier=-1)
    with p.block() as b:
        @b.vector
        def _(vec):
            vec.memset(p._ones[:], 1.0)
            vec.memset(p._ones1x128[:], 1.0)
            vec.memset(p._ones1x128f[:], 1.0)
            vec.tensor_single_scalar(out=p._idn_f32[:], in_=ii[:], scalar=0, op=ALU.is_equal)
            vec.drain()
            vec.tensor_copy(out=p._idn_bf[:], in_=p._idn_f32[:])


def build_flash_linear(nh, T, dq, dv, eps=1e-6, KC=512, name="flashlin"):
    p = Prog(name)
    qT = p.inp("qT", [nh, dq, T], BF16)
    kT = p.inp("kT", [nh, dq, T], BF16)
    vtm = p.inp("vtm", [nh, T, dv], BF16)
    y = p.out("y", [nh, T, dv])
    nqt = T // 128
    nkc = T // KC
    nvr = KC // 128

    qfm = p.sb("qfm", [128, T], BF16)
    kcb = [p.sb(f"kcb{i}", [128, KC], BF16) for i in range(2)]
    vcb = [p.sb(f"vcb{i}", [128, nvr * dv], BF16) for i in range(2)]
    acc = p.sb("acc", [128, nqt * dv], F32)
    den = p.sb("den", [128, nqt], F32)
    chs = p.sb("chs", [128, 1], F32)
    Est = p.sb("Est", [128, KC], BF16)
    ps_s = p.ps("ps_s", [128, KC], F32)
    ps_t = p.ps("ps_t", [128, KC], BF16)
    ps_c = p.ps("ps_c", [128, dv], F32)
    idn = p.sb("idn", [128, 128], BF16)
    idn_i = p.sb("idn_i", [128, 128], I32)
    msk01 = p.sb("msk01", [128, 2048], BF16)
    msk_i = p.sb("msk_i", [128, 2048], I32)
    sst = p.sem("sst")
    s_kv = [SC(p, "s_kv0"), SC(p, "s_kv1")]
    s_mm = SC(p, "s_mm")
    s_ex = SC(p, "s_ex")
    s_tr = SC(p, "s_tr")
    s_ec = SC(p, "s_ec")
    s_av = SC(p, "s_av")
    s_ac = SC(p, "s_ac")
    s_q = SC(p, "s_q")
    s_fin = SC(p, "s_fin")

    with p.block() as b:
        @b.gpsimd
        def _(gp):
            gp.iota(idn_i[:], pattern=[[1, 128]], base=0, channel_multiplier=-1)
            gp.iota(msk_i[:], pattern=[[1, 2048]], base=0, channel_multiplier=-1)

    with p.block() as b:
        @b.vector
        def _(vec):
            vec.tensor_single_scalar(out=idn[:], in_=idn_i[:], scalar=0, op=ALU.is_equal)
            vec.tensor_single_scalar(out=msk01[:], in_=msk_i[:], scalar=1024, op=ALU.is_le)

    for h in range(nh):
        with p.block() as b:
            @b.sync
            def _(sync, h=h):
                sq = sync.dma_start(out=qfm[0:dq, :], in_=qT[h])
                s_q.inc(sq, 16)
                sync.wait_ge(s_q.s, s_q.v)
                sc_ = sync.dma_start(out=kcb[0][0:dq, :], in_=kT[h][:, 0:KC])
                s_kv[0].inc(sc_, 16)
                for r in range(nvr):
                    sc_ = sync.dma_start(out=vcb[0][:, r * dv:(r + 1) * dv],
                                         in_=vtm[h][r * 128:(r + 1) * 128, :])
                    s_kv[0].inc(sc_, 16)

            @b.vector
            def _(vec):
                vec.memset(acc[:], 0.0)
                vec.memset(den[:], 0.0)

        for kc in range(nkc):
            cur = kc % 2
            nxt = (kc + 1) % 2
            qts = [qt for qt in range((kc * KC) // 128, nqt)]
            diag = {qt: (qt * 128 < kc * KC + KC) for qt in qts}
            B_mm = s_mm.v
            B_ex = s_ex.v
            B_tr = s_tr.v
            B_ec = s_ec.v
            B_av = s_av.v
            B_ac = s_ac.v
            with p.block() as b:
                B_kv = s_kv[cur].v
                if kc + 1 < nkc:
                    @b.sync
                    def _(sync, h=h, kc=kc, nxt=nxt):
                        o = (kc + 1) * KC
                        sc_ = sync.dma_start(out=kcb[nxt][0:dq, :], in_=kT[h][:, o:o + KC])
                        s_kv[nxt].inc(sc_, 16)
                        for r in range(nvr):
                            sc_ = sync.dma_start(out=vcb[nxt][:, r * dv:(r + 1) * dv],
                                                 in_=vtm[h][(o + r * 128):(o + (r + 1) * 128), :])
                            s_kv[nxt].inc(sc_, 16)

                @b.tensor
                def _(te, kc=kc, cur=cur, qts=qts, diag=diag, B_kv=B_kv,
                      B_mm=B_mm, B_ex=B_ex, B_ec=B_ec, B_ac=B_ac):
                    te.wait_ge(s_kv[cur].s, B_kv)
                    for i, qt in enumerate(qts):
                        te.wait_ge(s_ex.s, B_ex + i)
                        for hh_ in range(KC // 512):
                            sl_ = slice(hh_ * 512, (hh_ + 1) * 512)
                            mm = te.matmul(ps_s[:, sl_], qfm[0:dq, qt * 128:(qt + 1) * 128],
                                           kcb[cur][0:dq, sl_], start=True, stop=True)
                        mm.then_inc(s_mm.s, 1); s_mm.v += 1
                        te.wait_ge(s_ex.s, B_ex + i + 1)
                        te.wait_ge(s_ec.s, B_ec + i)
                        for r in range(nvr):
                            tr = te.transpose(ps_t[:, r * 128:(r + 1) * 128],
                                              Est[:, r * 128:(r + 1) * 128], idn[:])
                        tr.then_inc(s_tr.s, 1); s_tr.v += 1
                        te.wait_ge(s_ec.s, B_ec + i + 1)
                        te.wait_ge(s_ac.s, B_ac + i)
                        for r in range(nvr):
                            mm = te.matmul(ps_c[:], Est[:, r * 128:(r + 1) * 128],
                                           vcb[cur][:, r * dv:(r + 1) * dv],
                                           start=(r == 0), stop=(r == nvr - 1))
                        mm.then_inc(s_av.s, 1); s_av.v += 1

                @b.vector
                def _(vec, kc=kc, qts=qts, diag=diag, B_mm=B_mm, B_ex=B_ex,
                      B_tr=B_tr, B_av=B_av):
                    for i, qt in enumerate(qts):
                        vec.wait_ge(s_mm.s, B_mm + i + 1)
                        vec.wait_ge(s_av.s, B_av + i)
                        if diag[qt]:
                            d = qt * 128 - kc * KC
                            vec.tensor_tensor(out=Est[:], in0=ps_s[:],
                                              in1=msk01[:, 1024 - d:1024 - d + KC], op=ALU.mult)
                        else:
                            vec.tensor_copy(out=Est[:], in_=ps_s[:])
                        vec.drain()
                        ex = vec.reduce_sum(out=chs[:], in_=Est[:], axis=AX.X)
                        ex.then_inc(s_ex.s, 1); s_ex.v += 1
                        vec.drain()
                        vec.tensor_add(den[:, qt:qt + 1], den[:, qt:qt + 1], chs[:])
                        vec.wait_ge(s_tr.s, B_tr + i + 1)
                        ec = vec.tensor_copy(out=Est[:], in_=ps_t[:])
                        ec.then_inc(s_ec.s, 1); s_ec.v += 1
                        vec.wait_ge(s_av.s, B_av + i + 1)
                        au = vec.tensor_add(acc[:, qt * dv:(qt + 1) * dv],
                                            acc[:, qt * dv:(qt + 1) * dv], ps_c[:])
                        au.then_inc(s_ac.s, 1); s_ac.v += 1

        with p.block() as b:
            @b.vector
            def _(vec, h=h):
                vec.tensor_scalar_add(den[:], den[:], float(eps))
                vec.drain()
                vec.reciprocal(den[:], den[:])
                vec.drain()
                for qt in range(nqt):
                    vec.tensor_scalar_mul(acc[:, qt * dv:(qt + 1) * dv],
                                          acc[:, qt * dv:(qt + 1) * dv], den[:, qt:qt + 1])

        with p.block() as b:
            @b.sync
            def _(sync, h=h):
                for qt in range(nqt):
                    sf = sync.dma_start(out=y[h][qt * 128:(qt + 1) * 128, :],
                                        in_=acc[:, qt * dv:(qt + 1) * dv])
                    s_fin.inc(sf, 16)
                sync.wait_ge(s_fin.s, s_fin.v)

    with p.block() as b:
        @b.sync
        def _(sync):
            sync.wait_ge(s_kv[0].s, s_kv[0].v)
            sync.wait_ge(s_kv[1].s, s_kv[1].v)
    return p


def alloc_rms_scratch(p, T):
    return dict(
        x2s=p.sb("rms_x2s", [128, T], F32),
        stats=p.sb("rms_st", [128, 2 * (T // 128)], F32),
        rrow=p.sb("rms_rr", [1, T], F32),
        rfm=p.sb("rms_rf", [128, T], F32),
    )


def build_l1_qkv(T=2048, D=1024):
    p = Prog("l1qkv")
    xnT = p.inp("xnT", [D, T])
    qkv_w = p.inp("qkv_w", [D, 3 * D])
    qkv_b = p.inp("qkv_b", [128, 24])
    n1w = p.inp("n1w", [128, D // 128])
    cosw = p.inp("cosw", [128, T])
    sinw = p.inp("sinw", [128, T])
    q2T = p.out("q2T", [16, 128, T], BF16)
    k2T = p.out("k2T", [16, 128, T], BF16)
    vT = p.out("vT", [D, T], BF16)
    nf = D // 128
    sldg = p.sem("sldg")
    sld = p.sem("sld")
    sst = SC(p, "sst")
    clg = [0]
    cl = [0]

    emit_common_consts(p)
    xfm, xf = load_fm_bf16(p, "xfm", xnT, D, T, sldg, clg)
    cw = p.sb("cw", [128, 2 * T], BF16)
    with p.block() as b:
        @b.gpsimd
        def _(gp):
            gp.dma_start(out=cw[:, 0:T], in_=cosw[:, :]).then_inc(sldg, 16)
            gp.dma_start(out=cw[:, T:2 * T], in_=sinw[:, :]).then_inc(sldg, 16)
            clg[0] += 2
            gp.wait_ge(sldg, 16 * clg[0])
    _, n1c = load_scale_cols(p, "n1c", n1w, D, sld, cl)
    _, qbc = load_scale_cols(p, "qbc", qkv_b, 24 * 128, sld, cl)

    xn1 = p.sb("xn1", [128, nf * T], BF16)
    rs = alloc_rms_scratch(p, T)
    res = MMRes(p, max_nf=nf, tag="mm")
    sA = SC(p, "sA")
    sB = SC(p, "sB")
    emit_rmsnorm_fm(p, res, xf, lambda f: xn1[:, f * T:(f + 1) * T], T, D, n1c, 1e-6,
                    (sA, sB), rs["x2s"], rs["stats"], rs["rrow"], rs["rfm"])
    qkv = p.sb("qkv", [128, 24 * T], BF16)
    xn1f = lambda f: xn1[:, f * T:(f + 1) * T]
    linear_fm(p, res, xn1f, T, D, qkv_w, 3 * D,
              lambda c, t: qkv[:, c * T + t * res.TCH:], bias_col=qbc)

    stg = p.sb("stg", [128, T], BF16)
    for which in range(2):
        for h in range(16):
            ch = which * 8 + h // 2
            ro = (h % 2) * 64
            src = qkv[ro:ro + 64, ch * T:(ch + 1) * T]
            with p.block() as b:
                @b.vector
                def _(vec, src=src):
                    vec.tensor_relu(out=stg[0:64, :], in_=src)
                    vec.drain()
                    vec.tensor_tensor(out=stg[64:128, :], in0=stg[0:64, :], in1=cw[0:64, T:2 * T], op=ALU.mult)
                    vec.drain()
                    vec.tensor_tensor(out=stg[0:64, :], in0=stg[0:64, :], in1=cw[0:64, 0:T], op=ALU.mult)
            with p.block() as b:
                @b.sync
                def _(sync, which=which, h=h):
                    dst = q2T if which == 0 else k2T
                    s = sync.dma_start(out=dst[h], in_=stg[:])
                    sst.inc(s, 16)
                    sync.wait_ge(sst.s, sst.v)
    with p.block() as b:
        @b.sync
        def _(sync):
            for f in range(nf):
                s = sync.dma_start(out=vT[f * 128:(f + 1) * 128, :],
                                   in_=qkv[:, (16 + f) * T:(17 + f) * T])
                sst.inc(s, 16)
            sync.wait_ge(sst.s, sst.v)
    return p


def build_ffn_tail(T=2048, D=1024, F=4096, act=None, name="tail", TQ=512):
    p = Prog(name)
    yT = p.inp("yT", [D, T])
    xnT = p.inp("xnT", [D, T])
    wo = p.inp("wo", [D, D])
    bo = p.inp("bo", [128, D // 128])
    n2w = p.inp("n2w", [128, D // 128])
    w1 = p.inp("w1", [D, F])
    b1 = p.inp("b1", [128, F // 128])
    w2 = p.inp("w2", [F, D])
    b2 = p.inp("b2", [128, D // 128])
    oT = p.out("oT", [D, T])
    nf = D // 128
    nff = F // 128
    sld = p.sem("sld")
    sldg = p.sem("sldg")
    sst = SC(p, "sst")
    cl = [0]
    clg = [0]

    emit_common_consts(p)
    _, boc = load_scale_cols(p, "boc", bo, D, sld, cl)
    _, n2c = load_scale_cols(p, "n2c", n2w, D, sld, cl)
    _, b1c = load_scale_cols(p, "b1c", b1, F, sld, cl)
    _, b2c = load_scale_cols(p, "b2c", b2, D, sld, cl)

    ybf = p.sb("ybf", [128, nf * TQ], BF16)
    xbf = p.sb("xbf", [128, nf * TQ], BF16)
    hb = p.sb("hb", [128, nf * TQ], F32)
    hn = p.sb("hn", [128, nf * TQ], BF16)
    gb = p.sb("gb", [128, nff * TQ], BF16)
    ob = p.sb("ob", [128, nf * TQ], F32)
    rs = alloc_rms_scratch(p, TQ)
    res = MMRes(p, max_nf=nff, tag="mm")
    sA = SC(p, "sA")
    sB = SC(p, "sB")

    for q in range(T // TQ):
        t0 = q * TQ
        with p.block() as b:
            @b.gpsimd
            def _(gp, t0=t0):
                for f in range(nf):
                    gp.dma_start(out=ybf[:, f * TQ:(f + 1) * TQ],
                                 in_=yT[f * 128:(f + 1) * 128, t0:t0 + TQ]).then_inc(sldg, 16)
                    gp.dma_start(out=xbf[:, f * TQ:(f + 1) * TQ],
                                 in_=xnT[f * 128:(f + 1) * 128, t0:t0 + TQ]).then_inc(sldg, 16)
                    clg[0] += 2
                gp.wait_ge(sldg, 16 * clg[0])
        yf = lambda f: ybf[:, f * TQ:(f + 1) * TQ]
        linear_fm(p, res, yf, TQ, D, wo, D, lambda c, t: hb[:, c * TQ + t * res.TCH:], bias_col=boc)
        with p.block() as b:
            @b.vector
            def _(vec):
                for f in range(nf):
                    vec.tensor_add(hb[:, f * TQ:(f + 1) * TQ], hb[:, f * TQ:(f + 1) * TQ],
                                   xbf[:, f * TQ:(f + 1) * TQ])
                    vec.drain()
        hf = lambda f: hb[:, f * TQ:(f + 1) * TQ]
        emit_rmsnorm_fm(p, res, hf, lambda f: hn[:, f * TQ:(f + 1) * TQ], TQ, D, n2c, 1e-6,
                        (sA, sB), rs["x2s"], rs["stats"], rs["rrow"], rs["rfm"])
        hnf = lambda f: hn[:, f * TQ:(f + 1) * TQ]
        linear_fm(p, res, hnf, TQ, D, w1, F, lambda c, t: gb[:, c * TQ + t * res.TCH:],
                  act=act, bias_col=b1c)
        gf = lambda f: gb[:, f * TQ:(f + 1) * TQ]
        linear_fm(p, res, gf, TQ, F, w2, D, lambda c, t: ob[:, c * TQ + t * res.TCH:], bias_col=b2c)
        with p.block() as b:
            @b.vector
            def _(vec):
                for f in range(nf):
                    vec.tensor_add(ob[:, f * TQ:(f + 1) * TQ], ob[:, f * TQ:(f + 1) * TQ],
                                   hb[:, f * TQ:(f + 1) * TQ])
                    vec.drain()
        with p.block() as b:
            @b.sync
            def _(sync, t0=t0):
                for f in range(nf):
                    s = sync.dma_start(out=oT[f * 128:(f + 1) * 128, t0:t0 + TQ],
                                       in_=ob[:, f * TQ:(f + 1) * TQ])
                    sst.inc(s, 16)
                sync.wait_ge(sst.s, sst.v)
    return p


def build_rope_qkv(T=2048, D=1024, outs_spec=(("qT", 1024, True), ("kT", 256, True), ("vT", 256, False)),
                   name="ropeqkv", rms_weight=True, eps=1e-5):
    p = Prog(name)
    xnT = p.inp("xnT", [D, T])
    ln1 = p.inp("ln1", [128, D // 128]) if rms_weight else None
    ws = {nm: p.inp("w_" + nm, [D, dout]) for nm, dout, _ in outs_spec}
    cosr = p.inp("cosr", [128, T])
    sinr = p.inp("sinr", [128, T])
    outs = {nm: p.out(nm, [dout, T], BF16) for nm, dout, _ in outs_spec}
    nf = D // 128
    sld = p.sem("sld")
    sldg = p.sem("sldg")
    sst = SC(p, "sst")
    cl = [0]
    clg = [0]

    emit_common_consts(p)
    permT = emit_build_pairperm(p, "prm", 32, 64)
    xfm, xf = load_fm_bf16(p, "xfm", xnT, D, T, sldg, clg)
    cw = p.sb("cw", [128, 2 * T], BF16)
    with p.block() as b:
        @b.gpsimd
        def _(gp):
            gp.dma_start(out=cw[:, 0:T], in_=cosr[:, :]).then_inc(sldg, 16)
            gp.dma_start(out=cw[:, T:2 * T], in_=sinr[:, :]).then_inc(sldg, 16)
            clg[0] += 2
            gp.wait_ge(sldg, 16 * clg[0])
    lnc = None
    if rms_weight:
        _, lnc = load_scale_cols(p, "lnc", ln1, D, sld, cl)

    xn1 = p.sb("xn1", [128, nf * T], BF16)
    rs = alloc_rms_scratch(p, T)
    res = MMRes(p, max_nf=nf, tag="mm")
    sA = SC(p, "sA")
    sB = SC(p, "sB")
    emit_rmsnorm_fm(p, res, xf, lambda f: xn1[:, f * T:(f + 1) * T], T, D, lnc, eps,
                    (sA, sB), rs["x2s"], rs["stats"], rs["rrow"], rs["rfm"])
    xn1f = lambda f: xn1[:, f * T:(f + 1) * T]
    tot = sum(d for _, d, _ in outs_spec)
    buf = p.sb("buf", [128, (tot // 128) * T], BF16)
    off = 0
    bufs = {}
    for nm, dout, _ in outs_spec:
        bufs[nm] = (off, dout)
        linear_fm(p, res, xn1f, T, D, ws[nm], dout,
                  lambda c, t, off=off: buf[:, (off + c) * T + t * res.TCH:])
        off += dout // 128
    smm2 = SC(p, "smm2")
    sev2 = SC(p, "sev2")
    scrr = p.sb("scrr", [128, 512], F32)
    for nm, dout, do_rope in outs_spec:
        if not do_rope:
            continue
        off, _ = bufs[nm]
        emit_rope_pairs(p, res, permT, buf[:, off * T:(off + dout // 128) * T], dout // 128, T,
                        cw[:, 0:T], cw[:, T:2 * T], scrr, (smm2, sev2))
    with p.block() as b:
        @b.sync
        def _(sync):
            for nm, dout, _ in outs_spec:
                off, _ = bufs[nm]
                for c in range(dout // 128):
                    s = sync.dma_start(out=outs[nm][c * 128:(c + 1) * 128, :],
                                       in_=buf[:, (off + c) * T:(off + c + 1) * T])
                    sst.inc(s, 16)
            sync.wait_ge(sst.s, sst.v)
    return p


def build_moe_tail(T=2048, D=1024, F=2048, E=8, name="moetail"):
    p = Prog(name)
    yT = p.inp("yT", [D, T])
    xnT = p.inp("xnT", [D, T])
    Wo = p.inp("Wo", [D, D])
    ln2 = p.inp("ln2", [128, D // 128])
    Wr = p.inp("Wr", [D, E])
    Wg = p.inp("Wg", [E, D, F])
    Wu = p.inp("Wu", [E, D, F])
    Wd = p.inp("Wd", [E, F, D])
    oT = p.out("oT", [D, T])
    nf = D // 128
    nff = F // 128
    TQ = 512
    ntq = TQ // 128
    sld = p.sem("sld")
    sldg = p.sem("sldg")
    sst = SC(p, "sst")
    cl = [0]
    clg = [0]

    emit_common_consts(p)
    _, ln2c = load_scale_cols(p, "ln2c", ln2, D, sld, cl)

    ybf = p.sb("ybf", [128, nf * TQ], BF16)
    xbf = p.sb("xbf", [128, nf * TQ], BF16)
    hb = p.sb("hb", [128, nf * TQ], F32)
    hn = p.sb("hn", [128, nf * TQ], BF16)
    gb = p.sb("gb", [128, nff * TQ], BF16)
    ub = p.sb("ub", [128, nff * TQ], BF16)
    wsel = p.sb("wsel", [128, ntq * E], F32)
    wfm = p.sb("wfm", [128, TQ], F32)
    wrow = p.sb("wrow", [1, TQ], F32)
    scr8 = p.sb("scr8", [128, 4 * (TQ // 128)], F32)
    scrE = p.sb("scrE", [128, 1 + E], F32)
    moe_sb = p.sb("moe_sb", [128, (D // 128) * TQ], F32)
    rs = alloc_rms_scratch(p, TQ)
    res = MMRes(p, max_nf=nff, tag="mm")
    ps_moe = p.ps("ps_moe", [128, TQ], F32)
    sA = SC(p, "sA")
    sB = SC(p, "sB")
    sR = SC(p, "sR")

    for q in range(T // TQ):
        t0 = q * TQ
        with p.block() as b:
            @b.gpsimd
            def _(gp, t0=t0):
                for f in range(nf):
                    gp.dma_start(out=ybf[:, f * TQ:(f + 1) * TQ],
                                 in_=yT[f * 128:(f + 1) * 128, t0:t0 + TQ]).then_inc(sldg, 16)
                    gp.dma_start(out=xbf[:, f * TQ:(f + 1) * TQ],
                                 in_=xnT[f * 128:(f + 1) * 128, t0:t0 + TQ]).then_inc(sldg, 16)
                    clg[0] += 2
                gp.wait_ge(sldg, 16 * clg[0])
        yf = lambda f: ybf[:, f * TQ:(f + 1) * TQ]
        linear_fm(p, res, yf, TQ, D, Wo, D, lambda c, t: hb[:, c * TQ + t * res.TCH:])
        with p.block() as b:
            @b.vector
            def _(vec):
                for f in range(nf):
                    sl = slice(f * TQ, (f + 1) * TQ)
                    vec.tensor_add(hb[:, sl], hb[:, sl], xbf[:, sl])
                    vec.drain()
                    vec.tensor_scalar(out=hb[:, sl], in0=hb[:, sl], scalar1=100.0, scalar2=-100.0,
                                      op0=ALU.min, op1=ALU.max)
                    vec.drain()
        hf = lambda f: hb[:, f * TQ:(f + 1) * TQ]
        emit_rmsnorm_fm(p, res, hf, lambda f: hn[:, f * TQ:(f + 1) * TQ], TQ, D, ln2c, 1e-5,
                        (sA, sB), rs["x2s"], rs["stats"], rs["rrow"], rs["rfm"])
        with p.block() as b:
            @b.gpsimd
            def _(gp):
                for f in range(nf):
                    gp.dma_start(out=res.wt[:, f * E:(f + 1) * E],
                                 in_=Wr[f * 128:(f + 1) * 128, :]).then_inc(sldg, 16)
                    clg[0] += 1
                gp.wait_ge(sldg, 16 * clg[0])
        with p.block() as b:
            @b.tensor
            def _(te):
                for j in range(ntq):
                    for f in range(nf):
                        te.matmul(res.ps[0][:, j * E:(j + 1) * E],
                                  hn[:, f * TQ + j * 128:f * TQ + (j + 1) * 128],
                                  res.wt[:, f * E:(f + 1) * E],
                                  start=(f == 0), stop=(f == nf - 1)).then_inc(sR.s, 1)
                        sR.v += 1
        with p.block() as b:
            @b.vector
            def _(vec):
                for j in range(ntq):
                    lj = res.ps[0][:, j * E:(j + 1) * E]
                    m1 = scrE[:, 0:1]
                    tmp = scrE[:, 1:1 + E]
                    vec.reduce_max(out=m1, in_=lj, axis=AX.X)
                    vec.drain()
                    vec.tensor_scalar(out=tmp, in0=lj, scalar1=m1, scalar2=-1e30,
                                      op0=ALU.is_equal, op1=ALU.mult)
                    vec.drain()
                    vec.tensor_add(tmp, tmp, lj)
                    vec.drain()
                    vec.reduce_max(out=scr8[:, ntq + j:ntq + j + 1], in_=tmp, axis=AX.X)
                    vec.drain()
                    vec.tensor_sub(scr8[:, ntq + j:ntq + j + 1], scr8[:, ntq + j:ntq + j + 1], m1)
                    vec.drain()
                    vec.tensor_scalar_mul(scr8[:, j:j + 1], m1, -1.0)
                    vec.drain()
        with p.block() as b:
            @b.scalar
            def _(sca):
                for j in range(ntq):
                    sca.activation(out=wsel[:, j * E:(j + 1) * E], in_=res.ps[0][:, j * E:(j + 1) * E],
                                   func=AF.Exp, bias=scr8[:, j:j + 1])
                    sca.drain()
                sca.activation(out=scr8[:, 2 * ntq:3 * ntq], in_=scr8[:, ntq:2 * ntq], func=AF.Exp)
        with p.block() as b:
            @b.vector
            def _(vec):
                vec.tensor_scalar_add(scr8[:, 3 * ntq:4 * ntq], scr8[:, 2 * ntq:3 * ntq], 1.0)
                vec.drain()
                vec.reciprocal(scr8[:, 3 * ntq:4 * ntq], scr8[:, 3 * ntq:4 * ntq])
                vec.drain()
                for j in range(ntq):
                    wj = wsel[:, j * E:(j + 1) * E]
                    vec.tensor_single_scalar(out=scrE[:, 1:1 + E], in_=wj, scalar=scr8[:, 2 * ntq + j:2 * ntq + j + 1], op=ALU.is_ge)
                    vec.drain()
                    vec.tensor_mul(wj, wj, scrE[:, 1:1 + E])
                    vec.drain()
                    vec.tensor_scalar_mul(wj, wj, scr8[:, 3 * ntq + j:3 * ntq + j + 1])
                    vec.drain()

        with p.block() as b:
            @b.vector
            def _(vec):
                vec.memset(moe_sb[:], 0.0)
        for e in range(E):
            with p.block() as b:
                B_R = sR.v
                @b.tensor
                def _(te, e=e, B_R=B_R):
                    for j in range(ntq):
                        if j >= 1:
                            te.wait_ge(sR.s, B_R + j)
                        te.matmul(res.ps[0][0:1, 0:128], wsel[:, j * E + e:j * E + e + 1],
                                  p._idn_f32[:], start=True, stop=True).then_inc(sA.s, 1)
                        sA.v += 1
                B_A = sA.v

                @b.vector
                def _(vec, e=e, B_A=B_A):
                    m0 = B_A - ntq
                    for j in range(ntq):
                        vec.wait_ge(sA.s, m0 + j + 1)
                        cp = vec.tensor_copy(out=wrow[0:1, j * 128:(j + 1) * 128],
                                             in_=res.ps[0][0:1, 0:128])
                        cp.then_inc(sR.s, 1)
                        sR.v += 1
            with p.block() as b:
                B_R2 = sR.v
                @b.tensor
                def _(te, B_R2=B_R2):
                    te.matmul(res.ps[0][:, 0:512], p._ones1x128f[0:1, 0:128], wrow[0:1, 0:512],
                              start=True, stop=True).then_inc(sA.s, 1)
                    sA.v += 1
                B_A2 = sA.v

                @b.vector
                def _(vec, B_A2=B_A2):
                    vec.wait_ge(sA.s, B_A2)
                    vec.tensor_copy(out=wfm[:], in_=res.ps[0][:, 0:512])
            hnf = lambda f: hn[:, f * TQ:(f + 1) * TQ]
            linear_fm(p, res, hnf, TQ, D, Wg[e], F, lambda c, t: gb[:, c * TQ + t * res.TCH:], act=AF.Silu)
            linear_fm(p, res, hnf, TQ, D, Wu[e], F, lambda c, t: ub[:, c * TQ + t * res.TCH:])
            with p.block() as b:
                @b.vector
                def _(vec):
                    for f in range(nff):
                        sl = slice(f * TQ, (f + 1) * TQ)
                        vec.tensor_mul(gb[:, sl], gb[:, sl], ub[:, sl])
                        vec.drain()
                        vec.tensor_tensor(out=gb[:, sl], in0=gb[:, sl], in1=wfm[:], op=ALU.mult)
                        vec.drain()
            gf = lambda f: gb[:, f * TQ:(f + 1) * TQ]
            linear_fm(p, res, gf, TQ, F, Wd[e], D,
                      lambda c, t: moe_sb[:, c * TQ + t * res.TCH:], evac_add=True)

        with p.block() as b:
            @b.vector
            def _(vec):
                for f in range(nf):
                    sl = slice(f * TQ, (f + 1) * TQ)
                    vec.tensor_add(moe_sb[:, sl], moe_sb[:, sl], hb[:, sl])
                    vec.drain()
                    vec.tensor_scalar(out=moe_sb[:, sl], in0=moe_sb[:, sl], scalar1=100.0,
                                      scalar2=-100.0, op0=ALU.min, op1=ALU.max)
                    vec.drain()
        with p.block() as b:
            @b.sync
            def _(sync, t0=t0):
                for f in range(nf):
                    s = sync.dma_start(out=oT[f * 128:(f + 1) * 128, t0:t0 + TQ],
                                       in_=moe_sb[:, f * TQ:(f + 1) * TQ])
                    sst.inc(s, 16)
                sync.wait_ge(sst.s, sst.v)
    return p


def build_router_w(T=2048, D=1024):
    p = Prog("routerw")
    xnT = p.inp("xnT", [D, T])
    rw = p.inp("rw", [128, D // 128])
    rb = p.inp("rb", [128, 1])
    w = p.out("w", [T])
    nf = D // 128
    nt = T // 128
    sld = p.sem("sld")
    sldg = p.sem("sldg")
    sst = SC(p, "sst")
    cl = [0]
    clg = [0]
    emit_common_consts(p)
    xfm, xf = load_fm_bf16(p, "xfm", xnT, D, T, sldg, clg)
    rwt = p.sb("rwt", [128, nf], BF16)
    rbt = p.sb("rbt", [128, 1], F32)
    wtile = p.sb("wtile", [128, nt], F32)
    psw = p.ps("psw", [128, nt], F32)
    with p.block() as b:
        @b.gpsimd
        def _(gp):
            gp.dma_start(out=rwt[:], in_=rw[:, :]).then_inc(sldg, 16)
            clg[0] += 1
            gp.wait_ge(sldg, 16 * clg[0])

        @b.sync
        def _(sync):
            sync.dma_start(out=rbt[:], in_=rb[:, :]).then_inc(sld, 16)
            cl[0] += 1
            sync.wait_ge(sld, 16 * cl[0])
    with p.block() as b:
        @b.tensor
        def _(te):
            for t in range(nt):
                for f in range(nf):
                    te.matmul(psw[:, t:t + 1], xf(f)[:, t * 128:(t + 1) * 128], rwt[:, f:f + 1],
                              start=(f == 0), stop=(f == nf - 1))
    with p.block() as b:
        @b.vector
        def _(vec):
            vec.tensor_single_scalar(out=wtile[:], in_=psw[:, 0:nt], scalar=rbt[:, 0:1], op=ALU.add)
    with p.block() as b:
        @b.sync
        def _(sync):
            for t in range(nt):
                s = sync.dma_start(out=w[t * 128:(t + 1) * 128], in_=wtile[:, t:t + 1])
                sst.inc(s, 16)
            sync.wait_ge(sst.s, sst.v)
    return p


def build_bisect(B=2, S=8192, k=1024, iters=60, bound=100.0):
    p = Prog("bisect")
    wf = p.inp("wfull", [B, S])
    flags = p.out("flags", [B, S])
    fterm = p.out("fterm", [B, S])
    nc_ = S // 128
    sld = p.sem("sld")
    sst = SC(p, "sst")
    sT = SC(p, "sT")
    sV = SC(p, "sV")
    emit_common_consts(p)
    wt = p.sb("wt", [128, nc_], F32)
    cmpT = p.sb("cmpT", [128, nc_], F32)
    cnt = p.sb("cnt", [128, 1], F32)
    sc1 = p.sb("sc1", [1, 8], F32)
    midc = p.sb("midc", [128, 1], F32)
    psb = p.ps("psb", [128, 8], F32)
    cl = [0]

    for bb in range(B):
        with p.block() as b:
            @b.sync
            def _(sync, bb=bb):
                sync.dma_start(out=wt[:], in_=wf[bb].rearrange("(p c) -> p c", p=128)).then_inc(sld, 16)
                cl[0] += 1
                sync.wait_ge(sld, 16 * cl[0])

            @b.vector
            def _(vec):
                vec.memset(sc1[0:1, 0:1], -bound)
                vec.memset(sc1[0:1, 1:2], bound)
        for it in range(iters):
            with p.block() as b:
                @b.vector
                def _(vec):
                    vec.tensor_add(sc1[0:1, 2:3], sc1[0:1, 0:1], sc1[0:1, 1:2])
                    vec.drain()
                    vec.tensor_scalar_mul(sc1[0:1, 2:3], sc1[0:1, 2:3], 0.5)
                    vec.drain()
            with p.block() as b:
                @b.tensor
                def _(te):
                    te.matmul(psb[:, 0:1], p._ones1x128f[0:1, 0:128], sc1[0:1, 2:3],
                              start=True, stop=True).then_inc(sT.s, 1)
                    sT.v += 1

                @b.vector
                def _(vec):
                    vec.wait_ge(sT.s, sT.v)
                    vec.tensor_copy(out=midc[:], in_=psb[:, 0:1])
                    vec.drain()
                    vec.tensor_single_scalar(out=cmpT[:], in_=wt[:], scalar=midc[:, 0:1], op=ALU.is_gt)
                    vec.drain()
                    vec.reduce_sum(out=cnt[:], in_=cmpT[:], axis=AX.X).then_inc(sV.s, 1)
                    sV.v += 1
            with p.block() as b:
                @b.tensor
                def _(te):
                    te.matmul(psb[0:1, 1:2], cnt[:], p._ones[:],
                              start=True, stop=True).then_inc(sT.s, 1)
                    sT.v += 1

                @b.vector
                def _(vec):
                    vec.wait_ge(sT.s, sT.v)
                    vec.tensor_copy(out=sc1[0:1, 3:4], in_=psb[0:1, 1:2])
                    vec.drain()
                    vec.tensor_single_scalar(out=sc1[0:1, 4:5], in_=sc1[0:1, 3:4],
                                             scalar=float(k) - 0.5, op=ALU.is_gt)
                    vec.drain()
                    vec.tensor_sub(sc1[0:1, 5:6], sc1[0:1, 2:3], sc1[0:1, 0:1])
                    vec.drain()
                    vec.tensor_mul(sc1[0:1, 5:6], sc1[0:1, 5:6], sc1[0:1, 4:5])
                    vec.drain()
                    vec.tensor_add(sc1[0:1, 0:1], sc1[0:1, 0:1], sc1[0:1, 5:6])
                    vec.tensor_sub(sc1[0:1, 6:7], sc1[0:1, 1:2], sc1[0:1, 2:3])
                    vec.drain()
                    vec.tensor_mul(sc1[0:1, 6:7], sc1[0:1, 6:7], sc1[0:1, 4:5])
                    vec.drain()
                    vec.tensor_add(sc1[0:1, 1:2], sc1[0:1, 2:3], sc1[0:1, 6:7])
                    vec.drain()
        with p.block() as b:
            @b.tensor
            def _(te):
                te.matmul(psb[:, 0:1], p._ones1x128f[0:1, 0:128], sc1[0:1, 1:2],
                          start=True, stop=True).then_inc(sT.s, 1)
                sT.v += 1

            @b.vector
            def _(vec):
                vec.wait_ge(sT.s, sT.v)
                vec.tensor_copy(out=midc[:], in_=psb[:, 0:1])
                vec.drain()
                vec.tensor_single_scalar(out=wt[:], in_=wt[:], scalar=midc[:, 0:1], op=ALU.is_gt)
                vec.drain()
                vec.tensor_scalar(out=cmpT[:], in0=wt[:], scalar1=-1.0, scalar2=1e9,
                                  op0=ALU.add, op1=ALU.mult)
        with p.block() as b:
            @b.sync
            def _(sync, bb=bb):
                s = sync.dma_start(out=flags[bb].rearrange("(p c) -> p c", p=128), in_=wt[:])
                sst.inc(s, 16)
                s = sync.dma_start(out=fterm[bb].rearrange("(p c) -> p c", p=128), in_=cmpT[:])
                sst.inc(s, 16)
                sync.wait_ge(sst.s, sst.v)
    return p


def build_flash_datamask(nh=4, T=8192, dq=64, dv=64, scale=0.125, KC=1024, SB=4096, name="flashdm"):
    p = Prog(name)
    qT = p.inp("qT", [nh, dq, T], BF16)
    kT = p.inp("kT", [nh, dq, T], BF16)
    vtm = p.inp("vtm", [nh, T, dv], BF16)
    amask = p.inp("amask", [SB, SB])
    fterm_i = p.inp("fterm", [1, T])
    mcol_i = p.inp("mcol", [128, 1])
    y = p.out("y", [nh, T, dv])
    nqt = T // 128
    nkc = T // KC
    nvr = KC // 128

    qfm = p.sb("qfm", [128, T], BF16)
    kcb = [p.sb(f"kcb{i}", [128, KC], BF16) for i in range(2)]
    vcb = [p.sb(f"vcb{i}", [128, nvr * dv], BF16) for i in range(2)]
    mkb = [p.sb(f"mkb{i}", [128, KC], BF16) for i in range(2)]
    acc = p.sb("acc", [128, nqt * dv], F32)
    mrow = p.sb("mrow", [128, nqt], F32)
    den = p.sb("den", [128, nqt], F32)
    nms = p.sb("nms", [128, 1], F32)
    dm = p.sb("dm", [128, 1], F32)
    alpha = p.sb("alpha", [128, 1], F32)
    chs = p.sb("chs", [128, 1], F32)
    rm = p.sb("rm", [128, 1], F32)
    Est = p.sb("Est", [128, KC], BF16)
    fterm = p.sb("ftm", [1, T], BF16)
    mcol = p.sb("mcol_sb", [128, 1], F32)
    ps_s = p.ps("ps_s", [128, KC], F32)
    ps_t = p.ps("ps_t", [128, KC], BF16)
    ps_c = p.ps("ps_c", [128, dv], F32)
    idn = p.sb("idn", [128, 128], BF16)
    idn_i = p.sb("idn_i", [128, 128], I32)
    ones1 = p.sb("ones1", [1, 128], BF16)
    sld = p.sem("sld")
    sldg = p.sem("sldg")
    s_kv = [SC(p, "s_kv0"), SC(p, "s_kv1")]
    s_mk = [SC(p, "s_mk0"), SC(p, "s_mk1")]
    s_mm = SC(p, "s_mm")
    s_ex = SC(p, "s_ex")
    s_mx = SC(p, "s_mx")
    s_al = SC(p, "s_al")
    s_tr = SC(p, "s_tr")
    s_ec = SC(p, "s_ec")
    s_av = SC(p, "s_av")
    s_ac = SC(p, "s_ac")
    s_q = SC(p, "s_q")
    s_fin = SC(p, "s_fin")
    p.init_consts([float(scale)])

    with p.block() as b:
        @b.gpsimd
        def _(gp):
            gp.iota(idn_i[:], pattern=[[1, 128]], base=0, channel_multiplier=-1)
            gp.dma_start(out=fterm[:], in_=fterm_i[:, :]).then_inc(sldg, 16)
            gp.wait_ge(sldg, 16)

        @b.sync
        def _(sync):
            sync.dma_start(out=mcol[:], in_=mcol_i[:, :]).then_inc(sld, 16)
            sync.wait_ge(sld, 16)

    with p.block() as b:
        @b.vector
        def _(vec):
            vec.tensor_single_scalar(out=idn[:], in_=idn_i[:], scalar=0, op=ALU.is_equal)
            vec.memset(ones1[:], 1.0)

    def qts_for(kc):
        return [qt for qt in range(nqt) if (kc * KC <= qt * 128 + 127) or qt >= 31]

    for h in range(nh):
        with p.block() as b:
            @b.sync
            def _(sync, h=h):
                sq = sync.dma_start(out=qfm[0:dq, :], in_=qT[h])
                s_q.inc(sq, 16)
                sync.wait_ge(s_q.s, s_q.v)
                sc_ = sync.dma_start(out=kcb[0][0:dq, :], in_=kT[h][:, 0:KC])
                s_kv[0].inc(sc_, 16)
                for r in range(nvr):
                    sc_ = sync.dma_start(out=vcb[0][:, r * dv:(r + 1) * dv],
                                         in_=vtm[h][r * 128:(r + 1) * 128, :])
                    s_kv[0].inc(sc_, 16)

            @b.vector
            def _(vec):
                vec.memset(acc[:], 0.0)
                vec.memset(mrow[:], -3e38)
                vec.memset(den[:], 0.0)

        for kc in range(nkc):
            cur = kc % 2
            nxt = (kc + 1) % 2
            qts = qts_for(kc)
            with p.block() as b:
                B_kv = s_kv[cur].v
                B_mm = s_mm.v
                B_ex = s_ex.v
                B_mx = s_mx.v
                B_al = s_al.v
                B_tr = s_tr.v
                B_ec = s_ec.v
                B_av = s_av.v
                B_ac = s_ac.v
                need_mask = [qt for qt in qts if qt <= 31 and kc <= 3]
                if kc + 1 < nkc:
                    @b.sync
                    def _(sync, h=h, kc=kc, nxt=nxt):
                        o = (kc + 1) * KC
                        sc_ = sync.dma_start(out=kcb[nxt][0:dq, :], in_=kT[h][:, o:o + KC])
                        s_kv[nxt].inc(sc_, 16)
                        for r in range(nvr):
                            sc_ = sync.dma_start(out=vcb[nxt][:, r * dv:(r + 1) * dv],
                                                 in_=vtm[h][(o + r * 128):(o + (r + 1) * 128), :])
                            s_kv[nxt].inc(sc_, 16)

                B_mk = [s_mk[0].v, s_mk[1].v]
                if need_mask:
                    @b.gpsimd
                    def _(gp, kc=kc, need_mask=need_mask, qts=qts, B_mm=B_mm):
                        for i, qt in enumerate(need_mask):
                            par = i % 2
                            if i >= 2:
                                pos = qts.index(need_mask[i - 2])
                                gp.wait_ge(s_mm.s, B_mm + pos + 1)
                            g = gp.dma_start(out=mkb[par][:],
                                             in_=amask[qt * 128:(qt + 1) * 128, kc * KC:(kc + 1) * KC])
                            s_mk[par].inc(g, 16)


                @b.tensor
                def _(te, h=h, kc=kc, cur=cur, qts=qts, need_mask=need_mask, B_kv=B_kv,
                      B_mm=B_mm, B_ex=B_ex, B_mx=B_mx, B_ec=B_ec, B_ac=B_ac, B_mk=B_mk):
                    te.wait_ge(s_kv[cur].s, B_kv)
                    mki = {qt: i for i, qt in enumerate(need_mask)}
                    for i, qt in enumerate(qts):
                        te.wait_ge(s_ex.s, B_ex + i)
                        te.wait_ge(s_mx.s, B_mx + i)
                        for hh_ in range(KC // 512):
                            sl = slice(hh_ * 512, (hh_ + 1) * 512)
                            mm = te.matmul(ps_s[:, sl], qfm[0:dq, qt * 128:(qt + 1) * 128],
                                           kcb[cur][0:dq, sl], start=True, stop=False)
                            mm = te.matmul(ps_s[:, sl], ones1[:],
                                           fterm[0:1, kc * KC + hh_ * 512:kc * KC + (hh_ + 1) * 512],
                                           start=False, stop=qt not in mki)
                            if qt in mki:
                                j = mki[qt]
                                te.wait_ge(s_mk[j % 2].s, B_mk[j % 2] + 16 * (j // 2 + 1))
                                mm = te.matmul(ps_s[:, sl], idn[:], mkb[j % 2][:, sl],
                                               start=False, stop=True)
                        mm.then_inc(s_mm.s, 1); s_mm.v += 1
                        te.wait_ge(s_ex.s, B_ex + i + 1)
                        te.wait_ge(s_ec.s, B_ec + i)
                        for r in range(nvr):
                            tr = te.transpose(ps_t[:, r * 128:(r + 1) * 128],
                                              Est[:, r * 128:(r + 1) * 128], idn[:])
                        tr.then_inc(s_tr.s, 1); s_tr.v += 1
                        te.wait_ge(s_ec.s, B_ec + i + 1)
                        te.wait_ge(s_ac.s, B_ac + i)
                        for r in range(nvr):
                            mm = te.matmul(ps_c[:], Est[:, r * 128:(r + 1) * 128],
                                           vcb[cur][:, r * dv:(r + 1) * dv],
                                           start=(r == 0), stop=(r == nvr - 1))
                        mm.then_inc(s_av.s, 1); s_av.v += 1

                @b.vector
                def _(vec, kc=kc, qts=qts, B_mm=B_mm, B_ex=B_ex, B_al=B_al, B_tr=B_tr, B_av=B_av):
                    for i, qt in enumerate(qts):
                        vec.wait_ge(s_mm.s, B_mm + i + 1)
                        vec.wait_ge(s_ex.s, B_ex + i)
                        vec.wait_ge(s_al.s, B_al + i)
                        if qt == 31 and kc >= 4:
                            vec.tensor_single_scalar(out=ps_s[:], in_=ps_s[:], scalar=mcol[:, 0:1], op=ALU.add)
                            vec.drain()
                        vec.reduce_max(out=rm[:], in_=ps_s[:], axis=AX.X)
                        vec.drain()
                        vec.tensor_scalar_mul(rm[:], rm[:], float(scale))
                        vec.drain()
                        vec.tensor_max(nms[:], mrow[:, qt:qt + 1], rm[:])
                        vec.drain()
                        vec.tensor_sub(dm[:], mrow[:, qt:qt + 1], nms[:])
                        vec.drain()
                        vec.tensor_copy(out=mrow[:, qt:qt + 1], in_=nms[:])
                        vec.drain()
                        iv = vec.tensor_scalar_mul(nms[:], nms[:], -1.0)
                        iv.then_inc(s_mx.s, 1); s_mx.v += 1
                        vec.wait_ge(s_al.s, B_al + i + 1)
                        vec.wait_ge(s_ex.s, B_ex + i + 1)
                        vec.scalar_tensor_tensor(out=den[:, qt:qt + 1], in0=den[:, qt:qt + 1],
                                                 scalar=alpha[:], in1=chs[:], op0=ALU.mult, op1=ALU.add)
                        vec.wait_ge(s_tr.s, B_tr + i + 1)
                        ec = vec.tensor_copy(out=Est[:], in_=ps_t[:])
                        ec.then_inc(s_ec.s, 1); s_ec.v += 1
                        vec.wait_ge(s_av.s, B_av + i + 1)
                        au = vec.scalar_tensor_tensor(out=acc[:, qt * dv:(qt + 1) * dv],
                                                      in0=acc[:, qt * dv:(qt + 1) * dv],
                                                      scalar=alpha[:], in1=ps_c[:],
                                                      op0=ALU.mult, op1=ALU.add)
                        au.then_inc(s_ac.s, 1); s_ac.v += 1

                @b.scalar
                def _(sca, qts=qts, B_mx=B_mx, B_av=B_av):
                    for i, qt in enumerate(qts):
                        sca.wait_ge(s_mx.s, B_mx + i + 1)
                        al = sca.activation(out=alpha[:], in_=dm[:], func=AF.Exp)
                        al.then_inc(s_al.s, 1); s_al.v += 1
                        sca.wait_ge(s_av.s, B_av + i)
                        ex = sca.activation(out=Est[:], in_=ps_s[:], func=AF.Exp,
                                            scale=p.c(float(scale)), bias=nms[:], accum_out=chs[:])
                        ex.then_inc(s_ex.s, 1); s_ex.v += 1

        with p.block() as b:
            @b.vector
            def _(vec, h=h):
                vec.reciprocal(den[:], den[:])
                vec.drain()
                for qt in range(nqt):
                    vec.tensor_scalar_mul(acc[:, qt * dv:(qt + 1) * dv],
                                          acc[:, qt * dv:(qt + 1) * dv], den[:, qt:qt + 1])
        with p.block() as b:
            @b.sync
            def _(sync, h=h):
                for qt in range(nqt):
                    sf = sync.dma_start(out=y[h][qt * 128:(qt + 1) * 128, :],
                                        in_=acc[:, qt * dv:(qt + 1) * dv])
                    s_fin.inc(sf, 16)
                sync.wait_ge(s_fin.s, s_fin.v)

    with p.block() as b:
        @b.sync
        def _(sync):
            sync.wait_ge(s_kv[0].s, s_kv[0].v)
            sync.wait_ge(s_kv[1].s, s_kv[1].v)

        @b.gpsimd
        def _(gp):
            gp.wait_ge(s_mk[0].s, s_mk[0].v)
            gp.wait_ge(s_mk[1].s, s_mk[1].v)
    return p


def build_l1f_tail(T=2048, D=1024, F=4096, TQ=512):
    p = Prog("l1ftail")
    yT = p.inp("yT", [D, T])
    xnT = p.inp("xnT", [D, T])
    wo = p.inp("wo", [D, D])
    ln2 = p.inp("ln2", [128, D // 128])
    w1 = p.inp("w1", [D, F])
    w2 = p.inp("w2", [F, D])
    flagb = p.inp("flagb", [128, T])
    wrb = p.inp("wrb", [128, T])
    oT = p.out("oT", [D, T])
    nf = D // 128
    nff = F // 128
    sld = p.sem("sld")
    sldg = p.sem("sldg")
    sst = SC(p, "sst")
    cl = [0]
    clg = [0]

    emit_common_consts(p)
    _, n2c = load_scale_cols(p, "n2c", ln2, D, sld, cl)

    ybf = p.sb("ybf", [128, nf * TQ], BF16)
    xbf = p.sb("xbf", [128, nf * TQ], F32)
    hb = p.sb("hb", [128, nf * TQ], F32)
    hn = p.sb("hn", [128, nf * TQ], BF16)
    gb = p.sb("gb", [128, nff * TQ], BF16)
    ob = p.sb("ob", [128, nf * TQ], F32)
    fw_ = p.sb("fw_", [128, 2 * TQ], F32)
    rs = alloc_rms_scratch(p, TQ)
    res = MMRes(p, max_nf=nff, tag="mm")
    sA = SC(p, "sA")
    sB = SC(p, "sB")

    for q in range(T // TQ):
        t0 = q * TQ
        with p.block() as b:
            @b.gpsimd
            def _(gp, t0=t0):
                for f in range(nf):
                    gp.dma_start(out=ybf[:, f * TQ:(f + 1) * TQ],
                                 in_=yT[f * 128:(f + 1) * 128, t0:t0 + TQ]).then_inc(sldg, 16)
                    clg[0] += 1
                gp.wait_ge(sldg, 16 * clg[0])

            @b.sync
            def _(sync, t0=t0):
                for f in range(nf):
                    sync.dma_start(out=xbf[:, f * TQ:(f + 1) * TQ],
                                   in_=xnT[f * 128:(f + 1) * 128, t0:t0 + TQ]).then_inc(sld, 16)
                    cl[0] += 1
                sync.dma_start(out=fw_[:, 0:TQ], in_=flagb[:, t0:t0 + TQ]).then_inc(sld, 16)
                sync.dma_start(out=fw_[:, TQ:2 * TQ], in_=wrb[:, t0:t0 + TQ]).then_inc(sld, 16)
                cl[0] += 2
                sync.wait_ge(sld, 16 * cl[0])
        yf = lambda f: ybf[:, f * TQ:(f + 1) * TQ]
        linear_fm(p, res, yf, TQ, D, wo, D, lambda c, t: hb[:, c * TQ + t * res.TCH:])
        with p.block() as b:
            @b.vector
            def _(vec):
                for f in range(nf):
                    sl = slice(f * TQ, (f + 1) * TQ)
                    vec.tensor_add(hb[:, sl], hb[:, sl], xbf[:, sl])
                    vec.drain()
        hf = lambda f: hb[:, f * TQ:(f + 1) * TQ]
        emit_rmsnorm_fm(p, res, hf, lambda f: hn[:, f * TQ:(f + 1) * TQ], TQ, D, n2c, 1e-6,
                        (sA, sB), rs["x2s"], rs["stats"], rs["rrow"], rs["rfm"])
        hnf = lambda f: hn[:, f * TQ:(f + 1) * TQ]
        linear_fm(p, res, hnf, TQ, D, w1, F, lambda c, t: gb[:, c * TQ + t * res.TCH:],
                  act=AF.Gelu_apprx_tanh)
        gf = lambda f: gb[:, f * TQ:(f + 1) * TQ]
        linear_fm(p, res, gf, TQ, F, w2, D, lambda c, t: ob[:, c * TQ + t * res.TCH:])
        with p.block() as b:
            @b.vector
            def _(vec):
                for f in range(nf):
                    sl = slice(f * TQ, (f + 1) * TQ)
                    vec.tensor_add(ob[:, sl], ob[:, sl], hb[:, sl])
                    vec.drain()
                    vec.tensor_tensor(out=ob[:, sl], in0=ob[:, sl], in1=fw_[:, TQ:2 * TQ], op=ALU.mult)
                    vec.drain()
                    vec.tensor_sub(ob[:, sl], ob[:, sl], xbf[:, sl])
                    vec.drain()
                    vec.tensor_tensor(out=ob[:, sl], in0=ob[:, sl], in1=fw_[:, 0:TQ], op=ALU.mult)
                    vec.drain()
                    vec.tensor_add(ob[:, sl], ob[:, sl], xbf[:, sl])
                    vec.drain()
        with p.block() as b:
            @b.sync
            def _(sync, t0=t0):
                for f in range(nf):
                    s = sync.dma_start(out=oT[f * 128:(f + 1) * 128, t0:t0 + TQ],
                                       in_=ob[:, f * TQ:(f + 1) * TQ])
                    sst.inc(s, 16)
                sync.wait_ge(sst.s, sst.v)
    return p


def build_l3_proj(T=2048, D=1024, H=16, DK=64):
    p = Prog("l3proj")
    Th = T + 3
    _ = None
    xnTh = p.inp("xnTh", [D, Th])
    Wq = p.inp("Wq", [D, D])
    Wk = p.inp("Wk", [D, D])
    Wv = p.inp("Wv", [D, D])
    Wa = p.inp("Wa", [D, H])
    Wb = p.inp("Wb", [D, H])
    dtb = p.inp("dtb", [128, H])
    bbi = p.inp("bb", [128, H])
    Alog = p.inp("Alog", [128, H])
    wcq = p.inp("wcq", [128, (D // 128) * 4])
    wck = p.inp("wck", [128, (D // 128) * 4])
    qnT = p.out("qnT", [D, T], BF16)
    knT = p.out("knT", [D, T], BF16)
    vT = p.out("vT", [D, T], BF16)
    g_o = p.out("g", [T, H])
    beta_o = p.out("beta", [T, H])
    nf = D // 128
    nt = T // 128
    sld = p.sem("sld")
    sldg = p.sem("sldg")
    sst = SC(p, "sst")
    sA = SC(p, "sA")
    sB = SC(p, "sB")
    cl = [0]
    clg = [0]

    p.init_consts([-1.0, 1.0])
    emit_common_consts(p)
    xfm, xf = load_fm_bf16(p, "xfm", xnTh, D, Th, sldg, clg)
    smalls = p.sb("smalls", [128, 8 * H], F32)
    wcqt = p.sb("wcqt", [128, nf * 4], F32)
    wckt = p.sb("wckt", [128, nf * 4], F32)
    with p.block() as b:
        @b.sync
        def _(sync):
            sync.dma_start(out=smalls[:, 0:H], in_=dtb[:, :]).then_inc(sld, 16)
            sync.dma_start(out=smalls[:, H:2 * H], in_=bbi[:, :]).then_inc(sld, 16)
            sync.dma_start(out=smalls[:, 2 * H:3 * H], in_=Alog[:, :]).then_inc(sld, 16)
            sync.dma_start(out=wcqt[:], in_=wcq[:, :]).then_inc(sld, 16)
            sync.dma_start(out=wckt[:], in_=wck[:, :]).then_inc(sld, 16)
            cl[0] += 5
            sync.wait_ge(sld, 16 * cl[0])
    with p.block() as b:
        @b.scalar
        def _(sca):
            sca.activation(out=smalls[:, 3 * H:4 * H], in_=smalls[:, 2 * H:3 * H], func=AF.Exp)
    with p.block() as b:
        @b.vector
        def _(vec):
            vec.tensor_scalar_mul(smalls[:, 3 * H:4 * H], smalls[:, 3 * H:4 * H], -1.0)

    res = MMRes(p, max_nf=nf, tag="mm")
    qb = p.sb("qb", [128, nf * Th], BF16)
    on_ = p.sb("on_", [128, nf * T], BF16)
    scc = p.sb("scc", [128, 64], F32)
    x2c = p.sb("x2c", [128, 512], BF16)
    rsb = p.sb("rsb", [2, 512], BF16)
    gsum = p.ps("gsum", [2, 512], F32)
    g128 = p.sb("g128", [128, 2], BF16)
    g2 = p.sb("g2", [2, 128], BF16)
    gi = p.sb("gi", [128, 128], I32)
    rsg = p.sb("rsg", [2, 512], F32)
    with p.block() as b:
        @b.gpsimd
        def _(gp):
            gp.iota(gi[:], pattern=[[0, 2], [0, 64]], base=0, channel_multiplier=1)
    with p.block() as b:
        @b.vector
        def _(vec):
            vec.tensor_single_scalar(out=gi[:, 0:2], in_=gi[:, 0:2], scalar=6, op=ALU.arith_shift_right)
            vec.drain()
            vec.tensor_copy(out=scc[:, 0:2], in_=gi[:, 0:2])
            vec.drain()
            vec.memset(scc[:, 2:3], 0.0)
            vec.memset(scc[:, 3:4], 1.0)
            vec.drain()
            vec.tensor_single_scalar(out=scc[:, 4:5], in_=scc[:, 0:1], scalar=0.5, op=ALU.is_lt)
            vec.drain()
            vec.tensor_single_scalar(out=scc[:, 5:6], in_=scc[:, 0:1], scalar=0.5, op=ALU.is_gt)
            vec.drain()
            vec.tensor_copy(out=g128[:, 0:1], in_=scc[:, 4:5])
            vec.drain()
            vec.tensor_copy(out=g128[:, 1:2], in_=scc[:, 5:6])

    g2i = p.sb("g2i", [2, 128], I32)
    with p.block() as b:
        @b.gpsimd
        def _(gp):
            gp.iota(g2i[:], pattern=[[1, 2], [0, 64]], base=0, channel_multiplier=-1)
    with p.block() as b:
        @b.vector
        def _(vec):
            vec.tensor_single_scalar(out=g2[:], in_=g2i[:], scalar=0, op=ALU.is_equal)

    def conv_norm(win, wct, dst):
        linear_fm(p, res, xf, Th, D, win, D, lambda c, t: qb[:, c * Th + t * res.TCH:])
        with p.block() as b:
            @b.vector
            def _(vec):
                for f in range(nf):
                    y = on_[:, f * T:(f + 1) * T]
                    xsl = lambda i: qb[:, f * Th + i:f * Th + i + T]
                    vec.tensor_scalar_mul(y, xsl(0), wct[:, f * 4:f * 4 + 1])
                    vec.drain()
                    for i in range(1, 4):
                        vec.scalar_tensor_tensor(out=y, in0=xsl(i), scalar=wct[:, f * 4 + i:f * 4 + i + 1],
                                                 in1=y, op0=ALU.mult, op1=ALU.add)
                        vec.drain()
        with p.block() as b:
            @b.scalar
            def _(sca):
                for f in range(nf):
                    sca.activation(out=on_[:, f * T:(f + 1) * T], in_=on_[:, f * T:(f + 1) * T], func=AF.Silu)
                    sca.drain()
        for f in range(nf):
            for t in range(T // 512):
                with p.block() as b:
                    @b.scalar
                    def _(sca, f=f, t=t):
                        sq = sca.activation(out=x2c[:], in_=on_[:, f * T + t * 512:f * T + (t + 1) * 512],
                                            func=AF.Square)
                        sq.then_inc(sA.s, 1)
                        sA.v += 1
                    B_A = sA.v

                    @b.tensor
                    def _(te, B_A=B_A):
                        te.wait_ge(sA.s, B_A)
                        te.matmul(gsum[:], g128[:], x2c[:],
                                  start=True, stop=True).then_inc(sB.s, 1)
                        sB.v += 1
                    B_B = sB.v

                    @b.vector
                    def _(vec, B_B=B_B):
                        vec.wait_ge(sB.s, B_B)
                        vec.tensor_scalar(out=rsg[:], in0=gsum[:], scalar1=1.0, scalar2=1e-6,
                                          op0=ALU.mult, op1=ALU.add)
                        vec.drain()
                with p.block() as b:
                    @b.scalar
                    def _(sca):
                        sca.activation(out=rsg[:], in_=rsg[:], func=AF.Sqrt)
                with p.block() as b:
                    @b.vector
                    def _(vec):
                        vec.reciprocal(rsg[:], rsg[:])
                        vec.drain()
                        vec.tensor_copy(out=rsb[:], in_=rsg[:])
                with p.block() as b:
                    @b.tensor
                    def _(te):
                        te.matmul(res.ps[0][:, 0:512], g2[:], rsb[:],
                                  start=True, stop=True).then_inc(sB.s, 1)
                        sB.v += 1
                    B_B2 = sB.v

                    @b.vector
                    def _(vec, f=f, t=t, B_B2=B_B2):
                        vec.wait_ge(sB.s, B_B2)
                        sl = slice(f * T + t * 512, f * T + (t + 1) * 512)
                        vec.tensor_tensor(out=dst[:, sl], in0=on_[:, sl], in1=res.ps[0][:, 0:512], op=ALU.mult)
    qn = p.sb("qn", [128, nf * T], BF16)
    kn = p.sb("kn", [128, nf * T], BF16)
    conv_norm(Wq, wcqt, qn)
    conv_norm(Wk, wckt, kn)
    vb = p.sb("vb", [128, nf * T], BF16)
    xfT = lambda f: xfm[:, f * Th + 3:f * Th + 3 + T]
    linear_fm(p, res, xfT, T, D, Wv, D, lambda c, t: vb[:, c * T + t * res.TCH:])
    gbuf = p.sb("gbuf", [128, nt * H], F32)
    bbuf = p.sb("bbuf", [128, nt * H], F32)
    waT = p.sb("waT", [128, nf * H], BF16)
    wbT = p.sb("wbT", [128, nf * H], BF16)
    with p.block() as b:
        @b.gpsimd
        def _(gp):
            for f in range(nf):
                gp.dma_start(out=waT[:, f * H:(f + 1) * H], in_=Wa[f * 128:(f + 1) * 128, :]).then_inc(sldg, 16)
                gp.dma_start(out=wbT[:, f * H:(f + 1) * H], in_=Wb[f * 128:(f + 1) * 128, :]).then_inc(sldg, 16)
                clg[0] += 2
            gp.wait_ge(sldg, 16 * clg[0])
    for t in range(nt):
        with p.block() as b:
            @b.tensor
            def _(te, t=t):
                for f in range(nf):
                    te.matmul(res.ps[0][:, 0:H], xfT(f)[:, t * 128:(t + 1) * 128],
                              waT[:, f * H:(f + 1) * H], start=(f == 0), stop=(f == nf - 1))
                    te.matmul(res.ps[1][:, 0:H], xfT(f)[:, t * 128:(t + 1) * 128],
                              wbT[:, f * H:(f + 1) * H], start=(f == 0), stop=(f == nf - 1))
        with p.block() as b:
            @b.vector
            def _(vec, t=t):
                vec.tensor_add(scc[:, 0:H], res.ps[0][:, 0:H], smalls[:, 0:H])
                vec.drain()
                vec.tensor_add(scc[:, H:2 * H], res.ps[1][:, 0:H], smalls[:, H:2 * H])
                vec.drain()
                vec.tensor_relu(out=scc[:, 2 * H:3 * H], in_=scc[:, 0:H])
        with p.block() as b:
            @b.scalar
            def _(sca, t=t):
                sca.activation(out=scc[:, 3 * H:4 * H], in_=scc[:, 0:H], func=AF.Abs)
                sca.drain()
                sca.activation(out=scc[:, 3 * H:4 * H], in_=scc[:, 3 * H:4 * H], func=AF.Exp, scale=p.c(-1.0))
                sca.drain()
                sca.activation(out=scc[:, 3 * H:4 * H], in_=scc[:, 3 * H:4 * H], func=AF.Ln, bias=p.c(1.0))
                sca.drain()
                sca.activation(out=bbuf[:, t * H:(t + 1) * H], in_=scc[:, H:2 * H], func=AF.Sigmoid)
        with p.block() as b:
            @b.vector
            def _(vec, t=t):
                vec.tensor_add(scc[:, 2 * H:3 * H], scc[:, 2 * H:3 * H], scc[:, 3 * H:4 * H])
                vec.drain()
                vec.tensor_tensor(out=gbuf[:, t * H:(t + 1) * H], in0=scc[:, 2 * H:3 * H],
                                  in1=smalls[:, 3 * H:4 * H], op=ALU.mult)

    with p.block() as b:
        @b.sync
        def _(sync):
            for f in range(nf):
                s = sync.dma_start(out=qnT[f * 128:(f + 1) * 128, :], in_=qn[:, f * T:(f + 1) * T])
                sst.inc(s, 16)
                s = sync.dma_start(out=knT[f * 128:(f + 1) * 128, :], in_=kn[:, f * T:(f + 1) * T])
                sst.inc(s, 16)
                s = sync.dma_start(out=vT[f * 128:(f + 1) * 128, :], in_=vb[:, f * T:(f + 1) * T])
                sst.inc(s, 16)
            for t in range(nt):
                s = sync.dma_start(out=g_o[t * 128:(t + 1) * 128, :], in_=gbuf[:, t * H:(t + 1) * H])
                sst.inc(s, 16)
                s = sync.dma_start(out=beta_o[t * 128:(t + 1) * 128, :], in_=bbuf[:, t * H:(t + 1) * H])
                sst.inc(s, 16)
            sync.wait_ge(sst.s, sst.v)
    return p


def build_l3_scan(nh=4, T=8192, DK=64, C=256, CG=15, name="l3scan"):
    p = Prog(name)
    qnT = p.inp("qnT", [nh, DK, T], BF16)
    knT = p.inp("knT", [nh, DK, T], BF16)
    kntm = p.inp("kntm", [nh, T, DK], BF16)
    vtm = p.inp("vtm", [nh, T, DK], BF16)
    g_i = p.inp("g", [nh, T])
    b_i = p.inp("beta", [nh, T])
    lamsel = p.inp("lamsel", [64, nh])
    normw = p.inp("normw", [64, 1])
    oT = p.out("oT", [nh, DK, T])
    nch = T // C
    GRP = 8
    ngrp = nch // GRP
    TG = GRP * C
    ntt = C // 128

    sld = p.sem("sld")
    sldg = p.sem("sldg")
    sA = SC(p, "sA")
    sB = SC(p, "sB")
    sV = SC(p, "sV")
    sP = SC(p, "sP")
    sst = SC(p, "sst")
    cl = [0]
    clg = [0]
    p.init_consts([-1.0, 1.0])
    emit_common_consts(p)

    lam = p.sb("lam", [64, nh], F32)
    nw = p.sb("nw", [64, 1], F32)
    Hkk = p.sb("Hkk", [64, 64], F32)
    Hkv = p.sb("Hkv", [64, 64], F32)
    Hall_kk = p.sb("Hall_kk", [64, nch * 64], F32)
    Hall_kv = p.sb("Hall_kv", [64, nch * 64], F32)
    grow = p.sb("grow", [1, TG], F32)
    brow = p.sb("brow", [1, TG], F32)
    gcum = p.sb("gcum", [1, TG], F32)
    egtr = p.sb("egtr", [1, TG], F32)
    wrow = p.sb("wrow", [1, C], F32)
    col1 = p.sb("col1", [128, 8], F32)
    k32c = p.sb("k32c", [128, 2 * DK], F32)
    v32c = p.sb("v32c", [128, 2 * DK], F32)
    ktw = p.sb("ktw", [128, 2 * DK], F32)
    kt32 = p.sb("kt32", [128, GRP * ntt * DK], F32)
    vt32 = p.sb("vt32", [128, GRP * ntt * DK], F32)
    kfm = p.sb("kfm", [64, TG], F32)
    qfm = p.sb("qfm", [64, TG], F32)
    egfm = p.sb("egfm", [64, TG], F32)
    Wb_ = p.sb("Wb_", [128, GRP * ntt * C], F32)
    xb = p.sb("xb", [64, TG], F32)
    rb = p.sb("rb", [64, TG], F32)
    pb = p.sb("pb", [64, TG], F32)
    Apb = p.sb("Apb", [64, TG], F32)
    ptil = p.sb("ptil", [64, TG], F32)
    el = p.sb("el", [64, TG], F32)
    ws0 = p.sb("ws0", [128, C], F32)
    ws1 = p.sb("ws1", [128, C], F32)
    rs_row = p.sb("rs_row", [1, TG], F32)
    rsn_row = p.sb("rsn_row", [1, TG], F32)
    pap_row = p.sb("pap_row", [1, TG], F32)
    al_row = p.sb("al_row", [1, TG], F32)
    mskA = p.sb("mskA", [128, C], F32)
    mskB = p.sb("mskB", [128, C], F32)
    mski = p.sb("mski", [128, C], I32)
    ps_mv = p.ps("ps_mv", [64, C], F32)
    ps_S = p.ps("ps_S", [128, C], F32)
    ps_d = p.ps("ps_d", [1, 2048], F32)
    ps_w = p.ps("ps_w", [128, 8], F32)
    ps_b2 = p.ps("ps_b2", [64, 512], F32)

    with p.block() as b:
        @b.gpsimd
        def _(gp):
            gp.iota(mski[:], pattern=[[1, C]], base=0, channel_multiplier=-1)

        @b.sync
        def _(sync):
            sync.dma_start(out=lam[:], in_=lamsel[:, :]).then_inc(sld, 16)
            sync.dma_start(out=nw[:], in_=normw[:, :]).then_inc(sld, 16)
            cl[0] += 2
            sync.wait_ge(sld, 16 * cl[0])

    with p.block() as b:
        @b.vector
        def _(vec):
            vec.tensor_single_scalar(out=mskA[:], in_=mski[:], scalar=-1, op=ALU.is_gt)
            vec.tensor_single_scalar(out=mskB[:], in_=mski[:], scalar=127, op=ALU.is_gt)
            vec.drain()
            vec.tensor_scalar(out=mskA[:], in0=mskA[:], scalar1=-1.0, scalar2=1e30,
                              op0=ALU.add, op1=ALU.mult)
            vec.tensor_scalar(out=mskB[:], in0=mskB[:], scalar1=-1.0, scalar2=1e30,
                              op0=ALU.add, op1=ALU.mult)

    with p.block() as b:
        @b.scalar
        def _(sca):
            sca.activation(out=ws0[0:64, 0:nh], in_=lam[:], func=AF.Abs)
            sca.drain()
            sca.activation(out=ws0[0:64, 0:nh], in_=ws0[0:64, 0:nh], func=AF.Exp, scale=p.c(-1.0)[0:64])
            sca.drain()
            sca.activation(out=ws0[0:64, 0:nh], in_=ws0[0:64, 0:nh], func=AF.Ln, bias=p.c(1.0)[0:64])
    with p.block() as b:
        @b.vector
        def _(vec):
            vec.tensor_relu(out=ws1[0:64, 0:nh], in_=lam[:])
            vec.drain()
            vec.tensor_add(lam[:], ws0[0:64, 0:nh], ws1[0:64, 0:nh])
            vec.drain()
            vec.tensor_scalar_add(lam[:], lam[:], 0.25)

    for h in range(nh):
        with p.block() as b:
            @b.vector
            def _(vec):
                vec.memset(Hkk[:], 0.0)
                vec.memset(Hkv[:], 0.0)
        for c in range(nch):
            with p.block() as b:
                @b.sync
                def _(sync, h=h, c=c):
                    sync.dma_start(out=grow[0:1, 0:C],
                                   in_=g_i[h][c * C:(c + 1) * C].unsqueeze(0)).then_inc(sld, 16)
                    sync.dma_start(out=wrow[0:1, 0:C],
                                   in_=b_i[h][c * C:(c + 1) * C].unsqueeze(0)).then_inc(sld, 16)
                    cl[0] += 2
                    sync.wait_ge(sld, 16 * cl[0])

                @b.gpsimd
                def _(gp, h=h, c=c):
                    for tt in range(ntt):
                        gp.dma_start(out=k32c[:, tt * DK:(tt + 1) * DK],
                                     in_=kntm[h][(c * C + tt * 128):(c * C + (tt + 1) * 128), :]).then_inc(sldg, 16)
                        gp.dma_start(out=v32c[:, tt * DK:(tt + 1) * DK],
                                     in_=vtm[h][(c * C + tt * 128):(c * C + (tt + 1) * 128), :]).then_inc(sldg, 16)
                        clg[0] += 2
                    gp.wait_ge(sldg, 16 * clg[0])
            with p.block() as b:
                @b.vector
                def _(vec):
                    vec.tensor_tensor_scan(out=gcum[0:1, 0:C], data0=grow[0:1, 0:C],
                                           data1=grow[0:1, 0:C], initial=0.0,
                                           op0=ALU.add, op1=ALU.bypass)
                    vec.drain()
                    vec.tensor_scalar(out=egtr[0:1, 0:C], in0=gcum[0:1, 0:C], scalar1=-1.0,
                                      scalar2=gcum[0:1, C - 1:C], op0=ALU.mult, op1=ALU.add)
            with p.block() as b:
                @b.scalar
                def _(sca):
                    sca.activation(out=egtr[0:1, 0:C], in_=egtr[0:1, 0:C], func=AF.Exp)
                    sca.drain()
                    sca.activation(out=gcum[0:1, C - 1:C], in_=gcum[0:1, C - 1:C], func=AF.Exp)
            with p.block() as b:
                @b.vector
                def _(vec):
                    vec.tensor_mul(wrow[0:1, 0:C], wrow[0:1, 0:C], egtr[0:1, 0:C])
            with p.block() as b:
                @b.tensor
                def _(te):
                    for tt in range(ntt):
                        te.matmul(ps_w[:, tt:tt + 1], wrow[0:1, tt * 128:(tt + 1) * 128],
                                  p._ones1x128f[0:1, 0:1], start=True, stop=True)
                    mm = te.matmul(ps_w[:, ntt:ntt + 1], p._ones1x128f[0:1, 0:128],
                                   gcum[0:1, C - 1:C], start=True, stop=True)
                    mm.then_inc(sB.s, 1)
                    sB.v += 1
                B_B = sB.v

                @b.vector
                def _(vec, B_B=B_B):
                    vec.wait_ge(sB.s, B_B)
                    vec.tensor_copy(out=col1[:, 0:ntt + 1], in_=ps_w[:, 0:ntt + 1])
                    vec.drain()
                    for tt in range(ntt):
                        vec.tensor_scalar_mul(ktw[:, tt * DK:(tt + 1) * DK],
                                              k32c[:, tt * DK:(tt + 1) * DK], col1[:, tt:tt + 1])
                        vec.drain()
            with p.block() as b:
                @b.tensor
                def _(te):
                    for tt in range(ntt):
                        te.matmul(ps_mv[:, 0:DK], ktw[:, tt * DK:(tt + 1) * DK],
                                  k32c[:, tt * DK:(tt + 1) * DK], start=(tt == 0), stop=(tt == ntt - 1))
                    for tt in range(ntt):
                        mm = te.matmul(ps_S[0:64, 0:DK], ktw[:, tt * DK:(tt + 1) * DK],
                                       v32c[:, tt * DK:(tt + 1) * DK], start=(tt == 0), stop=(tt == ntt - 1))
                    mm.then_inc(sB.s, 1)
                    sB.v += 1
                B_B2 = sB.v

                @b.vector
                def _(vec, h=h, c=c, B_B2=B_B2):
                    vec.wait_ge(sB.s, B_B2)
                    vec.tensor_copy(out=Hall_kk[:, c * 64:(c + 1) * 64], in_=Hkk[:])
                    vec.tensor_copy(out=Hall_kv[:, c * 64:(c + 1) * 64], in_=Hkv[:])
                    vec.drain()
                    vec.scalar_tensor_tensor(out=Hkk[:], in0=Hkk[:], scalar=col1[0:64, ntt:ntt + 1],
                                             in1=ps_mv[:, 0:DK], op0=ALU.mult, op1=ALU.add)
                    vec.drain()
                    vec.scalar_tensor_tensor(out=Hkv[:], in0=Hkv[:], scalar=col1[0:64, ntt:ntt + 1],
                                             in1=ps_S[0:64, 0:DK], op0=ALU.mult, op1=ALU.add)
                    vec.drain()

        for grp in range(ngrp):
            g0 = grp * GRP
            t0 = g0 * C
            with p.block() as b:
                @b.sync
                def _(sync, h=h, t0=t0):
                    sync.dma_start(out=grow[0:1, :], in_=g_i[h][t0:t0 + TG].unsqueeze(0)).then_inc(sld, 16)
                    sync.dma_start(out=brow[0:1, :], in_=b_i[h][t0:t0 + TG].unsqueeze(0)).then_inc(sld, 16)
                    cl[0] += 2
                    sync.wait_ge(sld, 16 * cl[0])

                @b.gpsimd
                def _(gp, h=h, t0=t0):
                    gp.dma_start(out=qfm[:, :], in_=qnT[h][:, t0:t0 + TG]).then_inc(sldg, 16)
                    gp.dma_start(out=kfm[:, :], in_=knT[h][:, t0:t0 + TG]).then_inc(sldg, 16)
                    clg[0] += 2
                    for j in range(GRP * ntt):
                        gp.dma_start(out=kt32[:, j * DK:(j + 1) * DK],
                                     in_=kntm[h][(t0 + j * 128):(t0 + (j + 1) * 128), :]).then_inc(sldg, 16)
                        gp.dma_start(out=vt32[:, j * DK:(j + 1) * DK],
                                     in_=vtm[h][(t0 + j * 128):(t0 + (j + 1) * 128), :]).then_inc(sldg, 16)
                        clg[0] += 2
                    gp.wait_ge(sldg, 16 * clg[0])
            with p.block() as b:
                @b.vector
                def _(vec):
                    for j in range(GRP):
                        vec.tensor_tensor_scan(out=gcum[0:1, j * C:(j + 1) * C],
                                               data0=grow[0:1, j * C:(j + 1) * C],
                                               data1=grow[0:1, j * C:(j + 1) * C],
                                               initial=0.0, op0=ALU.add, op1=ALU.bypass)
                        vec.drain()
            with p.block() as b:
                @b.scalar
                def _(sca):
                    sca.activation(out=egtr[0:1, :], in_=gcum[0:1, :], func=AF.Exp)
            with p.block() as b:
                B_B3 = sB.v
                B_V3 = sV.v

                @b.tensor
                def _(te, B_B3=B_B3, B_V3=B_V3):
                    for s5 in range(TG // 512):
                        if s5 >= 1:
                            te.wait_ge(sV.s, B_V3 + s5)
                        mm = te.matmul(ps_b2[:, 0:512], p._ones1x128f[0:1, 0:64],
                                       egtr[0:1, s5 * 512:(s5 + 1) * 512], start=True, stop=True)
                        mm.then_inc(sB.s, 1)
                        sB.v += 1

                @b.vector
                def _(vec, B_B3=B_B3):
                    for s5 in range(TG // 512):
                        vec.wait_ge(sB.s, B_B3 + s5 + 1)
                        cp = vec.tensor_copy(out=egfm[:, s5 * 512:(s5 + 1) * 512], in_=ps_b2[:, 0:512])
                        cp.then_inc(sV.s, 1)
                        sV.v += 1
            for j in range(GRP):
                with p.block() as b:
                    @b.tensor
                    def _(te, j=j):
                        for tt in range(ntt):
                            te.matmul(ps_w[:, tt:tt + 1], gcum[0:1, j * C + tt * 128:j * C + (tt + 1) * 128],
                                      p._ones1x128f[0:1, 0:1], start=True, stop=True)
                            te.matmul(ps_w[:, ntt + tt:ntt + tt + 1], brow[0:1, j * C + tt * 128:j * C + (tt + 1) * 128],
                                      p._ones1x128f[0:1, 0:1], start=True, stop=True)
                        mm = te.matmul(ps_S[:, 0:C], p._ones1x128f[0:1, 0:128],
                                       gcum[0:1, j * C:(j + 1) * C], start=True, stop=True)
                        mm.then_inc(sB.s, 1)
                        sB.v += 1
                    B_B5 = sB.v

                    @b.vector
                    def _(vec, j=j, B_B5=B_B5):
                        vec.wait_ge(sB.s, B_B5)
                        vec.tensor_copy(out=col1[:, 0:2 * ntt], in_=ps_w[:, 0:2 * ntt])
                        vec.drain()
                        for tt in range(ntt):
                            w_t = Wb_[:, (j * ntt + tt) * C:(j * ntt + tt + 1) * C]
                            vec.tensor_single_scalar(out=w_t, in_=ps_S[:, 0:C],
                                                     scalar=col1[:, tt:tt + 1], op=ALU.subtract)
                            vec.drain()
                            vec.tensor_add(w_t, w_t, mskA[:] if tt == 0 else mskB[:])
                            vec.drain()
                with p.block() as b:
                    @b.scalar
                    def _(sca, j=j):
                        for tt in range(ntt):
                            w_t = Wb_[:, (j * ntt + tt) * C:(j * ntt + tt + 1) * C]
                            sca.activation(out=w_t, in_=w_t, func=AF.Exp)
                            sca.drain()
                with p.block() as b:
                    @b.vector
                    def _(vec, j=j):
                        for tt in range(ntt):
                            w_t = Wb_[:, (j * ntt + tt) * C:(j * ntt + tt + 1) * C]
                            vec.tensor_scalar_mul(w_t, w_t, col1[:, ntt + tt:ntt + tt + 1])
                            vec.drain()

            def emit_matvec(src_fm, src_til, Hall, dst_sb, lam_col):
                with p.block() as b:
                    B_A6 = sA.v
                    B_V6 = sV.v
                    B_B6 = sB.v
                    B_P6 = sP.v

                    @b.tensor
                    def _(te, B_A6=B_A6, B_V6=B_V6, B_P6=B_P6):
                        for j in range(GRP):
                            if j >= 1:
                                te.wait_ge(sP.s, B_P6 + j)
                            te.matmul(ps_mv[:, 0:C], Hall[:, (g0 + j) * 64:(g0 + j + 1) * 64],
                                      src_til[:, j * C:(j + 1) * C], start=True, stop=False)
                            for tt in range(ntt):
                                if j > 0 or tt > 0:
                                    te.wait_ge(sV.s, B_V6 + 2 * j + tt)
                                mm = te.matmul(ps_S[:, 0:C],
                                               kfm[:, j * C + tt * 128:j * C + (tt + 1) * 128],
                                               src_fm[:, j * C:(j + 1) * C],
                                               start=True, stop=True)
                                mm.then_inc(sA.s, 1)
                                sA.v += 1
                                te.wait_ge(sV.s, B_V6 + 2 * j + tt + 1)
                                mm = te.matmul(ps_mv[:, 0:C],
                                               kt32[:, (j * ntt + tt) * DK:(j * ntt + tt + 1) * DK] if Hall is Hall_kk else vt32[:, (j * ntt + tt) * DK:(j * ntt + tt + 1) * DK],
                                               ws0[:, 0:C] if tt == 0 else ws1[:, 0:C],
                                               start=False, stop=(tt == ntt - 1))
                            mm.then_inc(sB.s, 1)
                            sB.v += 1

                    @b.vector
                    def _(vec, B_A6=B_A6, B_B6=B_B6):
                        for j in range(GRP):
                            for tt in range(ntt):
                                vec.wait_ge(sA.s, B_A6 + 2 * j + tt + 1)
                                tv = vec.tensor_tensor(out=ws0[:, 0:C] if tt == 0 else ws1[:, 0:C],
                                                       in0=ps_S[:, 0:C],
                                                       in1=Wb_[:, (j * ntt + tt) * C:(j * ntt + tt + 1) * C],
                                                       op=ALU.mult)
                                tv.then_inc(sV.s, 1)
                                sV.v += 1
                            vec.wait_ge(sB.s, B_B6 + j + 1)
                            if lam_col is not None:
                                ap_ = vec.scalar_tensor_tensor(out=dst_sb[:, j * C:(j + 1) * C],
                                                               in0=src_fm[:, j * C:(j + 1) * C],
                                                               scalar=lam_col, in1=ps_mv[:, 0:C],
                                                               op0=ALU.mult, op1=ALU.add)
                            else:
                                ap_ = vec.tensor_copy(out=dst_sb[:, j * C:(j + 1) * C], in_=ps_mv[:, 0:C])
                            ap_.then_inc(sP.s, 1)
                            sP.v += 1
                            vec.drain()

            def emit_dots(src_a, src_b, dst_row):
                with p.block() as b:
                    @b.vector
                    def _(vec):
                        if src_b is None:
                            vec.tensor_mul(el[:], src_a[:], src_a[:])
                        else:
                            vec.tensor_mul(el[:], src_a[:], src_b[:])
                        vec.drain()
                with p.block() as b:
                    @b.tensor
                    def _(te):
                        for q4 in range(TG // 128):
                            mm = te.matmul(ps_d[0:1, q4 * 128:(q4 + 1) * 128],
                                           p._ones[0:64, 0:1], el[:, q4 * 128:(q4 + 1) * 128],
                                           start=True, stop=True)
                        mm.then_inc(sB.s, 1)
                        sB.v += 1
                    B_B7 = sB.v

                    @b.vector
                    def _(vec, B_B7=B_B7):
                        vec.wait_ge(sB.s, B_B7)
                        vec.tensor_copy(out=dst_row[0:1, :], in_=ps_d[0:1, 0:TG])
                        vec.drain()

            def emit_axpy(dst, a_row_sb, src, sign):
                with p.block() as b:
                    B_B8 = sB.v
                    B_V8 = sV.v

                    @b.tensor
                    def _(te, B_B8=B_B8, B_V8=B_V8):
                        for s5 in range(TG // 512):
                            if s5 >= 1:
                                te.wait_ge(sV.s, B_V8 + s5)
                            mm = te.matmul(ps_b2[:, 0:512], p._ones1x128f[0:1, 0:64],
                                           a_row_sb[0:1, s5 * 512:(s5 + 1) * 512], start=True, stop=True)
                            mm.then_inc(sB.s, 1)
                            sB.v += 1

                    @b.vector
                    def _(vec, B_B8=B_B8):
                        for s5 in range(TG // 512):
                            vec.wait_ge(sB.s, B_B8 + s5 + 1)
                            sl = slice(s5 * 512, (s5 + 1) * 512)
                            vec.tensor_tensor(out=el[:, sl], in0=src[:, sl], in1=ps_b2[:, 0:512],
                                              op=ALU.mult)
                            vec.drain()
                            tv = vec.tensor_tensor(out=dst[:, sl], in0=dst[:, sl], in1=el[:, sl],
                                                   op=ALU.add if sign > 0 else ALU.subtract)
                            tv.then_inc(sV.s, 1)
                            sV.v += 1
                            vec.drain()

            with p.block() as b:
                @b.vector
                def _(vec):
                    vec.memset(xb[:], 0.0)
                    vec.tensor_copy(out=rb[:], in_=qfm[:])
                    vec.drain()
                    vec.tensor_copy(out=pb[:], in_=qfm[:])
                    vec.drain()
            emit_dots(rb, None, rs_row)

            lamc = lam[:, h:h + 1]
            for it in range(CG):
                with p.block() as b:
                    @b.vector
                    def _(vec):
                        vec.tensor_mul(ptil[:], pb[:], egfm[:])
                        vec.drain()
                emit_matvec(pb, ptil, Hall_kk, Apb, lamc)
                emit_dots(pb, Apb, pap_row)
                with p.block() as b:
                    @b.vector
                    def _(vec):
                        vec.tensor_scalar_add(pap_row[0:1, :], pap_row[0:1, :], 1e-12)
                        vec.drain()
                        vec.reciprocal(pap_row[0:1, :], pap_row[0:1, :])
                        vec.drain()
                        vec.tensor_mul(al_row[0:1, :], rs_row[0:1, :], pap_row[0:1, :])
                        vec.drain()
                emit_axpy(xb, al_row, pb, +1)
                emit_axpy(rb, al_row, Apb, -1)
                emit_dots(rb, None, rsn_row)
                with p.block() as b:
                    @b.vector
                    def _(vec):
                        vec.tensor_scalar_add(al_row[0:1, :], rs_row[0:1, :], 1e-12)
                        vec.drain()
                        vec.reciprocal(al_row[0:1, :], al_row[0:1, :])
                        vec.drain()
                        vec.tensor_mul(al_row[0:1, :], al_row[0:1, :], rsn_row[0:1, :])
                        vec.drain()
                        vec.tensor_copy(out=rs_row[0:1, :], in_=rsn_row[0:1, :])
                        vec.drain()
                with p.block() as b:
                    B_B9 = sB.v
                    B_V9 = sV.v

                    @b.tensor
                    def _(te, B_B9=B_B9, B_V9=B_V9):
                        for s5 in range(TG // 512):
                            if s5 >= 1:
                                te.wait_ge(sV.s, B_V9 + s5)
                            mm = te.matmul(ps_b2[:, 0:512], p._ones1x128f[0:1, 0:64],
                                           al_row[0:1, s5 * 512:(s5 + 1) * 512], start=True, stop=True)
                            mm.then_inc(sB.s, 1)
                            sB.v += 1

                    @b.vector
                    def _(vec, B_B9=B_B9):
                        for s5 in range(TG // 512):
                            vec.wait_ge(sB.s, B_B9 + s5 + 1)
                            sl = slice(s5 * 512, (s5 + 1) * 512)
                            vec.tensor_tensor(out=pb[:, sl], in0=pb[:, sl], in1=ps_b2[:, 0:512],
                                              op=ALU.mult)
                            vec.drain()
                            tv = vec.tensor_add(pb[:, sl], pb[:, sl], rb[:, sl])
                            tv.then_inc(sV.s, 1)
                            sV.v += 1
                            vec.drain()

            with p.block() as b:
                @b.vector
                def _(vec):
                    vec.tensor_mul(ptil[:], xb[:], egfm[:])
                    vec.drain()
            emit_matvec(xb, ptil, Hall_kv, Apb, None)
            for q4 in range(TG // 128):
                with p.block() as b:
                    @b.vector
                    def _(vec, q4=q4):
                        vec.tensor_mul(el[:, q4 * 128:(q4 + 1) * 128], Apb[:, q4 * 128:(q4 + 1) * 128],
                                       Apb[:, q4 * 128:(q4 + 1) * 128])
                        vec.drain()
                with p.block() as b:
                    @b.tensor
                    def _(te, q4=q4):
                        mm = te.matmul(ps_w[:, 4:5], el[:, q4 * 128:(q4 + 1) * 128], p._ones[0:64, 0:1],
                                       start=True, stop=True)
                        mm.then_inc(sB.s, 1)
                        sB.v += 1
                    B_BA = sB.v

                    @b.vector
                    def _(vec, B_BA=B_BA):
                        vec.wait_ge(sB.s, B_BA)
                        vec.tensor_scalar(out=col1[:, 4:5], in0=ps_w[:, 4:5], scalar1=1.0 / 64,
                                          scalar2=1e-5, op0=ALU.mult, op1=ALU.add)
                        vec.drain()
                with p.block() as b:
                    @b.scalar
                    def _(sca):
                        sca.activation(out=col1[:, 5:6], in_=col1[:, 4:5], func=AF.Sqrt)
                with p.block() as b:
                    @b.vector
                    def _(vec):
                        vec.reciprocal(col1[:, 5:6], col1[:, 5:6])
                with p.block() as b:
                    @b.tensor
                    def _(te, q4=q4):
                        mm = te.matmul(ps_b2[0:1, 0:128], col1[:, 5:6], p._idn_f32[:],
                                       start=True, stop=True)
                        mm.then_inc(sB.s, 1)
                        sB.v += 1
                    B_BB = sB.v

                    @b.vector
                    def _(vec, B_BB=B_BB):
                        vec.wait_ge(sB.s, B_BB)
                        vec.tensor_copy(out=el[0:1, 0:128], in_=ps_b2[0:1, 0:128])
                        vec.drain()
                with p.block() as b:
                    @b.tensor
                    def _(te, q4=q4):
                        mm = te.matmul(ps_b2[:, 0:128], p._ones1x128f[0:1, 0:64], el[0:1, 0:128],
                                       start=True, stop=True)
                        mm.then_inc(sB.s, 1)
                        sB.v += 1
                    B_BC = sB.v

                    @b.vector
                    def _(vec, q4=q4, B_BC=B_BC):
                        vec.wait_ge(sB.s, B_BC)
                        sl = slice(q4 * 128, (q4 + 1) * 128)
                        vec.tensor_tensor(out=Apb[:, sl], in0=Apb[:, sl], in1=ps_b2[:, 0:128],
                                          op=ALU.mult)
                        vec.drain()
                        vec.tensor_scalar_mul(Apb[:, sl], Apb[:, sl], nw[:, 0:1])
                        vec.drain()
            with p.block() as b:
                @b.sync
                def _(sync, h=h, t0=t0):
                    s = sync.dma_start(out=oT[h][:, t0:t0 + TG], in_=Apb[0:64, :])
                    sst.inc(s, 16)
                    sync.wait_ge(sst.s, sst.v)
    return p



_PROGS = {}


def _dbg(name, arr):
    import os
    if not os.environ.get("KDBG"):
        return
    import numpy as _np
    try:
        want = _np.load(f"/root/problem/_oracle/{name}.npy").reshape(arr.shape)
        rl = _np.linalg.norm(_np.asarray(arr, _np.float64) - want) / (_np.linalg.norm(want) + 1e-30)
        print(f"[KDBG] {name}: rel_l2={rl:.4e}", flush=True)
    except Exception as e:
        print(f"[KDBG] {name}: ({e})", flush=True)


def _get(name, fn):
    if name not in _PROGS:
        _PROGS[name] = fn()
    return _PROGS[name]


def _cols(s):
    return np.ascontiguousarray(np.asarray(s, np.float32).reshape(-1, 128).T)


def _T(a):
    return np.ascontiguousarray(np.asarray(a).T)


def _rep(v, T_):
    return np.ascontiguousarray(np.broadcast_to(np.asarray(v, np.float32)[None, :], (128, T_)) if np.asarray(v).ndim == 1 else np.broadcast_to(np.asarray(v, np.float32), (128, T_)))


def _run8(p, in_maps):
    res = run_bass_kernel_spmd(p.nc, in_maps, core_ids=list(range(8)))
    return res.results


def kernel(**inp):
    N = 16384
    TS = 2048
    x = np.ascontiguousarray(np.asarray(inp["L0_attn_x"], np.float32).reshape(N, 1024))

    def shards(a):
        return [np.ascontiguousarray(a[c * TS:(c + 1) * TS]) for c in range(8)]

    p_ln0 = _get("ln0", lambda: build_add_ln(0))
    r = _run8(p_ln0, [{"x": s} for s in shards(x)])
    xn = np.concatenate([q["xn"] for q in r], 0)
    _dbg("xn0", xn)

    inv16 = 1.0 / 10000.0 ** (np.arange(16) / 16.0)
    rows16 = inv16[np.arange(128) % 16]

    p_l0p = _get("l0p", lambda: build_l0_proj())
    im = []
    for c in range(8):
        pos = np.arange((c % 4) * TS, (c % 4) * TS + TS)
        ang = pos[None, :] * rows16[:, None]
        im.append({"xnT": _T(xn[c * TS:(c + 1) * TS]),
                   "wq": inp["L0_attn_Wq"], "wqr": inp["L0_attn_Wqr"], "wkr": inp["L0_attn_Wkr"],
                   "wkvd": inp["L0_attn_Wkvd"], "wku": inp["L0_attn_Wku"], "wvu": inp["L0_attn_Wvu"],
                   "cosq": np.cos(ang).astype(np.float32), "sinq": np.sin(ang).astype(np.float32)})
    r = _run8(p_l0p, im)
    qc = np.concatenate([q["qcT"].T for q in r], 0).reshape(2, 8192, 16, 64)
    qr = np.concatenate([q["qrT"].T for q in r], 0).reshape(2, 8192, 16, 32)
    kr = np.concatenate([q["krT"][:32].T for q in r], 0).reshape(2, 8192, 32)
    kc = np.concatenate([q["kcT"].T for q in r], 0).reshape(2, 8192, 16, 64)
    vv = np.concatenate([q["vT"].T for q in r], 0).reshape(2, 8192, 16, 64)
    lobo = np.asarray(inp["L0_attn_lobo_log"], np.float32)

    p_l0f = _get("l0f", lambda: build_flash_heads(4, 8192, 96, 64, 1.0 / np.sqrt(96), causal=True, lobo=True, KC=1024))
    im = []
    for c in range(8):
        qs, ks, vs, ls = [], [], [], []
        for j in range(4):
            bh = 4 * c + j
            b_, h_ = bh // 16, bh % 16
            qs.append(np.concatenate([qc[b_, :, h_, :], qr[b_, :, h_, :]], -1).T)
            ks.append(np.concatenate([kc[b_, :, h_, :], kr[b_]], -1).T)
            vs.append(vv[b_, :, h_, :])
            ls.append(lobo[h_])
        im.append({"qT": np.ascontiguousarray(np.stack(qs)), "kT": np.ascontiguousarray(np.stack(ks)),
                   "vtm": np.ascontiguousarray(np.stack(vs)),
                   "lobo": np.ascontiguousarray(np.broadcast_to(np.array(ls, np.float32), (128, 4)))})
    r = _run8(p_l0f, im)
    y0 = np.stack([q["y"] for q in r]).reshape(2, 16, 8192, 64).transpose(0, 2, 1, 3).reshape(N, 1024)

    p_lin = _get("lin1024", lambda: build_linear())
    r = _run8(p_lin, [{"xT": _T(y0[c * TS:(c + 1) * TS]), "w": inp["L0_attn_Wo"]} for c in range(8)])
    attn0 = np.concatenate([q["yT"].T for q in r], 0)
    _dbg("attn0", attn0)

    p_l0ff = _get("l0ff", lambda: build_l0_ffn())
    r = _run8(p_l0ff, [{"xn": np.ascontiguousarray(xn[c * TS:(c + 1) * TS]),
                        "wg": inp["L0_ffn_w_gate"], "wu": inp["L0_ffn_w_up"],
                        "wd": inp["L0_ffn_w_down"]} for c in range(8)])
    ffn0 = np.concatenate([q["y"] for q in r], 0)
    _dbg("ffn0", ffn0)

    p_aln = _get("aln", lambda: build_add_ln(2))
    r = _run8(p_aln, [{"x": np.ascontiguousarray(x[c * TS:(c + 1) * TS]),
                       "a": np.ascontiguousarray(attn0[c * TS:(c + 1) * TS]),
                       "f": np.ascontiguousarray(ffn0[c * TS:(c + 1) * TS])} for c in range(8)])
    x = np.concatenate([q["xnew"] for q in r], 0)
    xn = np.concatenate([q["xn"] for q in r], 0)
    _dbg("x1", x)
    _dbg("xn1", xn)

    posf = np.arange(8192, dtype=np.float64)
    cw = np.cos(np.pi / 2 * posf / 8192).astype(np.float32)
    sw = np.sin(np.pi / 2 * posf / 8192).astype(np.float32)
    p_l1q = _get("l1q", lambda: build_l1_qkv())
    im = []
    for c in range(8):
        sl = slice((c % 4) * TS, (c % 4) * TS + TS)
        im.append({"xnT": _T(xn[c * TS:(c + 1) * TS]), "qkv_w": inp["L1_attn_qkv_w"],
                   "qkv_b": _cols(inp["L1_attn_qkv_b"]), "n1w": _cols(inp["L1_attn_norm1_w"]),
                   "cosw": _rep(cw[sl], TS), "sinw": _rep(sw[sl], TS)})
    r = _run8(p_l1q, im)
    q2 = np.concatenate([q["q2T"].transpose(1, 0, 2).reshape(128, TS * 16) for q in r], 1) if False else None
    q2a = np.stack([q["q2T"] for q in r])
    k2a = np.stack([q["k2T"] for q in r])
    v1 = np.concatenate([q["vT"].T for q in r], 0).reshape(2, 8192, 16, 64)

    p_l1f = _get("l1f", lambda: build_flash_linear(4, 8192, 128, 64, KC=1024))
    im = []
    for c in range(8):
        qs, ks, vs = [], [], []
        for j in range(4):
            bh = 4 * c + j
            b_, h_ = bh // 16, bh % 16
            qs.append(np.concatenate([q2a[4 * b_ + i, h_] for i in range(4)], axis=1))
            ks.append(np.concatenate([k2a[4 * b_ + i, h_] for i in range(4)], axis=1))
            vs.append(v1[b_, :, h_, :])
        im.append({"qT": np.ascontiguousarray(np.stack(qs)), "kT": np.ascontiguousarray(np.stack(ks)),
                   "vtm": np.ascontiguousarray(np.stack(vs))})
    r = _run8(p_l1f, im)
    y1 = np.stack([q["y"] for q in r]).reshape(2, 16, 8192, 64).transpose(0, 2, 1, 3).reshape(N, 1024)

    p_l1t = _get("l1t", lambda: build_ffn_tail(act=AF.Gelu_apprx_tanh))
    r = _run8(p_l1t, [{"yT": _T(y1[c * TS:(c + 1) * TS]), "xnT": _T(xn[c * TS:(c + 1) * TS]),
                       "wo": inp["L1_attn_out_w"], "bo": _cols(inp["L1_attn_out_b"]),
                       "n2w": _cols(inp["L1_attn_norm2_w"]), "w1": inp["L1_attn_mlp_w1"],
                       "b1": _cols(inp["L1_attn_mlp_b1"]), "w2": inp["L1_attn_mlp_w2"],
                       "b2": _cols(inp["L1_attn_mlp_b2"])} for c in range(8)])
    attn1 = np.concatenate([q["oT"].T for q in r], 0)
    _dbg("attn1", attn1)

    p_rw = _get("rw", lambda: build_router_w())
    r = _run8(p_rw, [{"xnT": _T(xn[c * TS:(c + 1) * TS]), "rw": _cols(np.asarray(inp["L1_ffn_router_w"]).reshape(-1)),
                      "rb": np.ascontiguousarray(np.broadcast_to(np.asarray(inp["L1_ffn_router_b"], np.float32).reshape(1, 1), (128, 1)))} for c in range(8)])
    wfull = np.concatenate([q["w"] for q in r], 0).reshape(2, 8192)

    p_bi = _get("bi", lambda: build_bisect())
    r = _run8(p_bi, [{"wfull": wfull} for _ in range(8)])
    flags = r[0]["flags"]
    fterm = r[0]["fterm"]

    inv32 = 1.0 / 10000.0 ** (np.arange(0, 64, 2) / 64.0)
    rows32 = inv32[np.arange(128) % 32]
    pos_cl = np.minimum(np.arange(8192), 4095)

    p_r2 = _get("r2", lambda: build_rope_qkv(outs_spec=(("qT", 1024, True), ("kT", 1024, True), ("vT", 1024, False)),
                                             name="l1fproj", eps=1e-6))
    im = []
    for c in range(8):
        pos = pos_cl[(c % 4) * TS:(c % 4) * TS + TS]
        ang = pos[None, :] * rows32[:, None]
        im.append({"xnT": _T(xn[c * TS:(c + 1) * TS]), "ln1": _cols(inp["L1_ffn_ln1"]),
                   "w_qT": inp["L1_ffn_wq"], "w_kT": inp["L1_ffn_wk"], "w_vT": inp["L1_ffn_wv"],
                   "cosr": np.cos(ang).astype(np.float32), "sinr": np.sin(ang).astype(np.float32)})
    r = _run8(p_r2, im)
    q1f = np.concatenate([q["qT"].T for q in r], 0).reshape(2, 8192, 16, 64)
    k1f = np.concatenate([q["kT"].T for q in r], 0).reshape(2, 8192, 16, 64)
    v1f = np.concatenate([q["vT"].T for q in r], 0).reshape(2, 8192, 16, 64)

    amask = np.asarray(inp["L1_ffn_attention_mask"], np.float32)
    p_dm = _get("dm", lambda: build_flash_datamask())
    im = []
    for c in range(8):
        qs, ks, vs = [], [], []
        for j in range(4):
            bh = 4 * c + j
            b_, h_ = bh // 16, bh % 16
            qs.append(q1f[b_, :, h_, :].T)
            ks.append(k1f[b_, :, h_, :].T)
            vs.append(v1f[b_, :, h_, :])
        b_ = (4 * c) // 16
        im.append({"qT": np.ascontiguousarray(np.stack(qs)), "kT": np.ascontiguousarray(np.stack(ks)),
                   "vtm": np.ascontiguousarray(np.stack(vs)),
                   "amask": np.ascontiguousarray(amask[b_, 0]),
                   "fterm": np.ascontiguousarray(fterm[b_][None, :]),
                   "mcol": np.ascontiguousarray(amask[b_, 0, 3968:4096, 4095:4096])})
    r = _run8(p_dm, im)
    o1f = np.stack([q["y"] for q in r]).reshape(2, 16, 8192, 64).transpose(0, 2, 1, 3).reshape(N, 1024)

    flags_f = flags.reshape(N)
    w_f = wfull.reshape(N)
    p_l1ft = _get("l1ft", lambda: build_l1f_tail())
    r = _run8(p_l1ft, [{"yT": _T(o1f[c * TS:(c + 1) * TS]), "xnT": _T(xn[c * TS:(c + 1) * TS]),
                        "wo": inp["L1_ffn_wo"], "ln2": _cols(inp["L1_ffn_ln2"]),
                        "w1": inp["L1_ffn_w1"], "w2": inp["L1_ffn_w2"],
                        "flagb": _rep(flags_f[c * TS:(c + 1) * TS], TS),
                        "wrb": _rep(w_f[c * TS:(c + 1) * TS], TS)} for c in range(8)])
    ffn1 = np.concatenate([q["oT"].T for q in r], 0)
    _dbg("ffn1", ffn1)

    r = _run8(p_aln, [{"x": np.ascontiguousarray(x[c * TS:(c + 1) * TS]),
                       "a": np.ascontiguousarray(attn1[c * TS:(c + 1) * TS]),
                       "f": np.ascontiguousarray(ffn1[c * TS:(c + 1) * TS])} for c in range(8)])
    x = np.concatenate([q["xnew"] for q in r], 0)
    xn = np.concatenate([q["xn"] for q in r], 0)
    _dbg("x2", x)

    HEAD_SIZE = 512
    pos = np.arange(8192)
    invh = 10000.0 ** (np.arange(0, HEAD_SIZE, 2) / HEAD_SIZE)
    rad = pos[:, None] / invh
    sincos = np.stack((np.sin(rad), np.cos(rad)), -1).reshape(8192, -1)
    t2 = np.repeat(sincos, 2, -1).astype(np.float32)

    p_l2p = _get("l2p", lambda: build_l2_proj())
    im = []
    for c in range(8):
        sl = slice((c % 4) * TS, (c % 4) * TS + TS)
        im.append({"xnT": _T(xn[c * TS:(c + 1) * TS]), "wqT": _T(inp["L2_attn_Wq"]),
                   "wkT": _T(inp["L2_attn_Wk"]), "wvT": _T(inp["L2_attn_Wv"]),
                   "bq": _cols(inp["L2_attn_bq"]), "bk": _cols(inp["L2_attn_bk"]),
                   "bv": _cols(inp["L2_attn_bv"]), "tfm": _T(t2[sl])})
    r = _run8(p_l2p, im)
    qT2 = [q["qT"] for q in r]
    kT2 = [q["kT"] for q in r]
    vT2 = [q["vT"] for q in r]

    p_l2f = _get("l2f", lambda: build_l2_flash())
    im = []
    for c in range(8):
        b_ = c // 4
        kfull = np.concatenate([kT2[4 * b_ + j] for j in range(4)], axis=1)
        vfull = np.concatenate([vT2[4 * b_ + j] for j in range(4)], axis=1)
        im.append({"qT": np.ascontiguousarray(qT2[c]), "kT": np.ascontiguousarray(kfull),
                   "vtm": np.ascontiguousarray(vfull.T)})
    r = _run8(p_l2f, im)
    attn2 = np.concatenate([q["y"] for q in r], 0)
    _dbg("attn2", attn2)

    p_f1 = _get("f1", lambda: build_rope_qkv(outs_spec=(("qT", 1024, True), ("kT", 256, True), ("vT", 256, False)),
                                             name="l2fproj", eps=1e-5))
    im = []
    for c in range(8):
        pos_ = np.arange((c % 4) * TS, (c % 4) * TS + TS)
        ang = pos_[None, :] * rows32[:, None]
        im.append({"xnT": _T(xn[c * TS:(c + 1) * TS]), "ln1": _cols(inp["L2_ffn_ln1_w"]),
                   "w_qT": inp["L2_ffn_Wq"], "w_kT": inp["L2_ffn_Wk"], "w_vT": inp["L2_ffn_Wv"],
                   "cosr": np.cos(ang).astype(np.float32), "sinr": np.sin(ang).astype(np.float32)})
    r = _run8(p_f1, im)
    q2f = np.concatenate([q["qT"].T for q in r], 0).reshape(2, 8192, 16, 64)
    k2f = np.concatenate([q["kT"].T for q in r], 0).reshape(2, 8192, 4, 64)
    v2f = np.concatenate([q["vT"].T for q in r], 0).reshape(2, 8192, 4, 64)

    p_gqa = _get("gqa", lambda: build_flash_heads(4, 8192, 64, 64, 0.125, causal=True, lobo=False, KC=1024))
    im = []
    for c in range(8):
        qs, ks, vs = [], [], []
        for j in range(4):
            bh = 4 * c + j
            b_, h_ = bh // 16, bh % 16
            qs.append(q2f[b_, :, h_, :].T)
            ks.append(k2f[b_, :, h_ // 4, :].T)
            vs.append(v2f[b_, :, h_ // 4, :])
        im.append({"qT": np.ascontiguousarray(np.stack(qs)), "kT": np.ascontiguousarray(np.stack(ks)),
                   "vtm": np.ascontiguousarray(np.stack(vs))})
    r = _run8(p_gqa, im)
    ctx2 = np.stack([q["y"] for q in r]).reshape(2, 16, 8192, 64).transpose(0, 2, 1, 3).reshape(N, 1024)

    p_moe = _get("moe", lambda: build_moe_tail())
    r = _run8(p_moe, [{"yT": _T(ctx2[c * TS:(c + 1) * TS]), "xnT": _T(xn[c * TS:(c + 1) * TS]),
                       "Wo": inp["L2_ffn_Wo"], "ln2": _cols(inp["L2_ffn_ln2_w"]),
                       "Wr": inp["L2_ffn_Wr"], "Wg": inp["L2_ffn_Wg"], "Wu": inp["L2_ffn_Wu"],
                       "Wd": inp["L2_ffn_Wd"]} for c in range(8)])
    ffn2 = np.concatenate([q["oT"].T for q in r], 0)
    _dbg("ffn2", ffn2)

    r = _run8(p_aln, [{"x": np.ascontiguousarray(x[c * TS:(c + 1) * TS]),
                       "a": np.ascontiguousarray(attn2[c * TS:(c + 1) * TS]),
                       "f": np.ascontiguousarray(ffn2[c * TS:(c + 1) * TS])} for c in range(8)])
    x = np.concatenate([q["xnew"] for q in r], 0)
    xn = np.concatenate([q["xn"] for q in r], 0)
    _dbg("x3", x)

    p_l3p = _get("l3p", lambda: build_l3_proj())
    im = []
    for c in range(8):
        if c % 4 == 0:
            halo = np.zeros((3, 1024), np.float32)
        else:
            halo = xn[c * TS - 3:c * TS]
        xnh = np.concatenate([halo, xn[c * TS:(c + 1) * TS]], 0)
        im.append({"xnTh": _T(xnh), "Wq": inp["L3_attn_Wq"], "Wk": inp["L3_attn_Wk"],
                   "Wv": inp["L3_attn_Wv"], "Wa": inp["L3_attn_Wa"], "Wb": inp["L3_attn_Wb"],
                   "dtb": _rep(inp["L3_attn_dt_bias"], 16), "bb": _rep(inp["L3_attn_bb"], 16),
                   "Alog": _rep(inp["L3_attn_A_log"], 16),
                   "wcq": np.ascontiguousarray(np.asarray(inp["L3_attn_Wconv_q"], np.float32).reshape(8, 128, 4).transpose(1, 0, 2).reshape(128, 32)),
                   "wck": np.ascontiguousarray(np.asarray(inp["L3_attn_Wconv_k"], np.float32).reshape(8, 128, 4).transpose(1, 0, 2).reshape(128, 32))})
    r = _run8(p_l3p, im)
    qn3 = np.concatenate([q["qnT"].T for q in r], 0).reshape(2, 8192, 16, 64)
    kn3 = np.concatenate([q["knT"].T for q in r], 0).reshape(2, 8192, 16, 64)
    v3 = np.concatenate([q["vT"].T for q in r], 0).reshape(2, 8192, 16, 64)
    g3 = np.concatenate([q["g"] for q in r], 0).reshape(2, 8192, 16)
    b3 = np.concatenate([q["beta"] for q in r], 0).reshape(2, 8192, 16)

    lam2 = np.ascontiguousarray(np.asarray(inp["L3_attn_lambda_params"], np.float32).reshape(16, 64).T)
    p_l3s = _get("l3s", lambda: build_l3_scan())
    im = []
    for c in range(8):
        qs, ks, kts, vts, gs, bs, lams = [], [], [], [], [], [], []
        for j in range(4):
            bh = 4 * c + j
            b_, h_ = bh // 16, bh % 16
            qs.append(qn3[b_, :, h_, :].T)
            ks.append(kn3[b_, :, h_, :].T)
            kts.append(kn3[b_, :, h_, :])
            vts.append(v3[b_, :, h_, :])
            gs.append(g3[b_, :, h_])
            bs.append(b3[b_, :, h_])
            lams.append(lam2[:, h_])
        im.append({"qnT": np.ascontiguousarray(np.stack(qs)), "knT": np.ascontiguousarray(np.stack(ks)),
                   "kntm": np.ascontiguousarray(np.stack(kts)), "vtm": np.ascontiguousarray(np.stack(vts)),
                   "g": np.ascontiguousarray(np.stack(gs).astype(np.float32)),
                   "beta": np.ascontiguousarray(np.stack(bs).astype(np.float32)),
                   "lamsel": np.ascontiguousarray(np.stack(lams, 1)),
                   "normw": np.ascontiguousarray(np.asarray(inp["L3_attn_norm_w"], np.float32)[:, None])})
    r = _run8(p_l3s, im)
    o3 = np.stack([q["oT"] for q in r]).reshape(2, 16, 64, 8192).transpose(0, 3, 1, 2).reshape(N, 1024)

    r = _run8(p_lin, [{"xT": _T(o3[c * TS:(c + 1) * TS]), "w": inp["L3_attn_Wo"]} for c in range(8)])
    attn3 = np.concatenate([q["yT"].T for q in r], 0)
    _dbg("attn3", attn3)

    p_l3ff = _get("l3ff", lambda: build_l3_ffn())
    shared = {"w1T": _T(inp["L3_ffn_w1_w"]), "w2T": _T(inp["L3_ffn_w2_w"]),
              "h1T": _T(inp["L3_ffn_h1_w"]), "h2T": _T(inp["L3_ffn_h2_w"]),
              "w3T": _T(inp["L3_ffn_w3_w"]), "s1": _cols(inp["L3_ffn_w1_s"]), "s2": _cols(inp["L3_ffn_w2_s"]),
              "sh1": _cols(inp["L3_ffn_h1_s"]), "sh2": _cols(inp["L3_ffn_h2_s"]), "s3": _cols(inp["L3_ffn_w3_s"])}
    im = []
    for c in range(8):
        d = dict(shared)
        d["xnT"] = _T(xn[c * TS:(c + 1) * TS])
        im.append(d)
    r = _run8(p_l3ff, im)
    ffn3 = np.concatenate([q["yT"].T for q in r], 0)
    _dbg("ffn3", ffn3)

    r = _run8(p_aln, [{"x": np.ascontiguousarray(x[c * TS:(c + 1) * TS]),
                       "a": np.ascontiguousarray(attn3[c * TS:(c + 1) * TS]),
                       "f": np.ascontiguousarray(ffn3[c * TS:(c + 1) * TS])} for c in range(8)])
    x = np.concatenate([q["xnew"] for q in r], 0)
    return np.ascontiguousarray(x.reshape(2, 8192, 1024).astype(np.float32))
